# version 04
# baseline (speedup 1.0000x reference)
import numpy as np
import ml_dtypes
from contextlib import ExitStack
import concourse.bass as bass
import concourse.mybir as mybir
from concourse.bass_utils import run_bass_kernel_spmd

F32 = mybir.dt.float32
BF16 = mybir.dt.bfloat16
AF = mybir.ActivationFunctionType
ALU = mybir.AluOpType
AX = mybir.AxisListType
BF = ml_dtypes.bfloat16

E_ = 2048
H_ = 16
T_ = 2048
SCALER = float(1.0 / np.sqrt(128.0))
EPS = 1e-5
RNE = 12582912.0
PI = float(np.pi)
SQRT2 = float(np.sqrt(2.0))
INIT_LAMBDA = 0.8

DEBUG = False
TRUNC = None
_CACHE = {}


class Sem:
    def __init__(self, stack, nc, name):
        self.h = stack.enter_context(nc.semaphore(name))
        self.n = 0


class Prog:
    ENGS = ("sync", "scalar", "vector", "tensor", "gpsimd")

    def __init__(self, stack, nc):
        self.nc = nc
        self.esem = {e: Sem(stack, nc, f"es_{e}") for e in self.ENGS}
        self.qsem = {e: Sem(stack, nc, f"qs_{e}") for e in ("sync", "scalar", "gpsimd")}
        self.pend = {e: [] for e in self.ENGS}

    def step(self, eng, waits=()):
        sem = self.esem[eng]

        def deco(fn):
            sem.n += 1
            self.pend[eng].append((list(waits), fn, sem, 1))
            return (sem, sem.n)
        return deco

    def dma(self, eng, waits=(), n=1):
        sem = self.qsem[eng]

        def deco(fn):
            sem.n += 16 * n
            self.pend[eng].append((list(waits), fn, sem, 16))
            return (sem, sem.n)
        return deco

    def barrier(self):
        toks = [(s, s.n) for s in self.esem.values() if s.n > 0]
        toks += [(s, s.n) for s in self.qsem.values() if s.n > 0]
        return toks

    def flush(self, block):
        for e in self.ENGS:
            steps = self.pend[e]
            if not steps:
                continue
            dec = getattr(block, e)
            own = self.esem[e]

            @dec
            def _(eng, steps=steps, own=own):
                for waits, fn, sem, inc in steps:
                    for (ws, wv) in waits:
                        if ws is not own:
                            eng.wait_ge(ws.h, wv)
                    out = fn(eng)
                    if isinstance(out, list):
                        for ins in out:
                            ins.then_inc(sem.h, inc)
                    else:
                        out.then_inc(sem.h, inc)
            self.pend[e] = []


def kcls(kc, qs):
    if kc >= 16:
        return 2
    if kc < qs * 4:
        return 0
    if kc < qs * 4 + 4:
        return 3
    return 1


def build_program():
    nc = bass.Bass("TRN2", target_bir_lowering=False)

    def din(name, shape, dt=F32):
        return nc.dram_tensor(name, shape, dt, kind="ExternalInput").ap()

    x0 = din("x0", [T_, E_])
    x1 = din("x1", [T_, E_])
    wqkv = din("wqkv", [3 * E_, E_])
    wo_w = din("wo_w", [E_, E_])
    lamv = din("lamv", [128, 4])
    grep = din("grep", [1, E_])
    brep = din("brep", [1, E_])
    ampw = din("ampw", [E_, E_])
    ampb = din("ampb", [1, E_])
    phw = din("phw", [E_, E_])
    phb = din("phb", [1, E_])
    entw = din("entw", [2, E_, 4 * E_])
    entb = din("entb", [2, 4 * E_])
    entpw = din("entpw", [2, 4 * E_, E_])
    entpb = din("entpb", [2, E_])
    pathw = din("pathw", [2, 2, E_, E_])
    pathb = din("pathb", [2, 2, E_])
    phasev = din("phasev", [2, 2, E_])
    l3g = din("l3g", [2, E_])
    l3b = din("l3b", [2, E_])
    l4g = din("l4g", [2, E_])
    l4b = din("l4b", [2, E_])
    ropetab = din("ropetab", [4, 128, 4096])
    mbands = din("mbands", [4, 128, 512], BF16)
    mo_in = din("mo", [1, 1])
    idbf = din("idbf", [128, 128], BF16)
    onesf_in = din("onesf_in", [128, 128])
    onesb_in = din("onesb_in", [128, 512], BF16)
    mtrot = din("mtrot", [128, 128], BF16)

    y_out = nc.dram_tensor("y", [T_, E_], F32, kind="ExternalOutput").ap()
    skind = "ExternalOutput" if DEBUG else "Internal"

    def dscr(name, shape, dt):
        return nc.dram_tensor(name, shape, dt, kind=skind).ap()

    wqkv_bf = dscr("wqkv_bf", [3 * E_, E_], BF16)
    wo_bf = dscr("wo_bf", [E_, E_], BF16)
    a_bf = dscr("a_bf", [2, 2064, 4 * E_], BF16)
    qscr = dscr("qscr", [H_, 128, T_], BF16)
    kscr = dscr("kscr", [H_, 128, 4096], BF16)
    vscr = dscr("vscr", [H_, 128, 4096], BF16)
    attnscr = dscr("attnscr", [T_, E_], BF16)
    r1scr = dscr("r1scr", [T_, E_], F32)
    cscr = dscr("cscr", [2, E_, E_], BF16)
    weffscr = dscr("weffscr", [2, E_, E_], BF16)
    dbgxn = dscr("dbgxn", [128, 16 * 2048], BF16) if DEBUG else None
    dbgln = dscr("dbgln", [128, 16 * 8], F32) if DEBUG else None

    root = ExitStack()
    with root:
        def sb(stk, name, shape, dt):
            return stk.enter_context(nc.sbuf_tensor(name, shape, dt))

        def ps(stk, name, shape, dt=F32):
            return stk.enter_context(nc.psum_tensor(name, shape, dt))

        block = root.enter_context(nc.Block())
        P = Prog(root, nc)

        ones_f = sb(root, "ones_f", [128, 128], F32)
        ones_b = sb(root, "ones_b", [128, 512], BF16)
        id_bf = sb(root, "id_bf", [128, 128], BF16)
        mt_rot = sb(root, "mt_rot", [128, 128], BF16)
        lam_sb = sb(root, "lam_sb", [128, 4], F32)
        lam_col = sb(root, "lam_col", [128, 1], F32)
        mo_col = sb(root, "mo_col", [128, 1], F32)
        grep_c = sb(root, "grep_c", [128, 16], F32)
        brep_c = sb(root, "brep_c", [128, 16], F32)
        ampb_c = sb(root, "ampb_c", [128, 16], F32)
        phb_c = sb(root, "phb_c", [128, 16], F32)
        lng_c = sb(root, "lng_c", [128, 64], F32)
        lnb_c = sb(root, "lnb_c", [128, 64], F32)
        entbias_r = sb(root, "entbias_r", [1, 2 * E_], BF16)
        pathbias_r = sb(root, "pathbias_r", [1, 2 * E_], BF16)
        rowa = sb(root, "rowa", [1, 2048], F32)
        tinyr = sb(root, "tinyr", [1, 64], F32)
        nbias = sb(root, "nbias", [128, 8], F32)
        lnst = sb(root, "lnst", [128, 8], F32)
        dbgln_sb = sb(root, "dbgln_sb", [128, 128], F32)
        stat4 = sb(root, "stat4", [128, 4], F32)
        mcol = sb(root, "mcol", [128, 1], F32)
        eps_c = sb(root, "eps_c", [128, 1], F32)
        t38_c = sb(root, "t38_c", [128, 1], F32)
        rcol = sb(root, "rcol", [128, 1], F32)
        sccol = sb(root, "sccol", [128, 2], F32)
        sct = sb(root, "sct", [128, 512], F32)
        sct2 = sb(root, "sct2", [128, 512], F32)

        p0 = ExitStack()
        pc = ps(p0, "pc", [128, 64], F32)

        cast_jobs = []
        for i in range(6):
            cast_jobs.append((wqkv_bf[i * 1024:(i + 1) * 1024, :], wqkv[i * 1024:(i + 1) * 1024, :]))
        cast_jobs.append((wo_bf[0:1024, :], wo_w[0:1024, :]))
        cast_jobs.append((wo_bf[1024:2048, :], wo_w[1024:2048, :]))
        for l in range(2):
            for i in range(4):
                cast_jobs.append((a_bf[l, i * 512:(i + 1) * 512, :], entw[l, i * 512:(i + 1) * 512, :]))
            cast_jobs.append((a_bf[l, 2048:2049, :], entb[l:l + 1, :]))

        t_cast = P.dma("gpsimd", n=len(cast_jobs))(
            lambda g, jobs=cast_jobs: [g.dma_start(out=o, in_=i) for (o, i) in jobs])
        tok_cast_wqkv = (P.qsem["gpsimd"], 6 * 16)
        tok_cast_wo = (P.qsem["gpsimd"], 8 * 16)
        tok_cast_a = t_cast

        t_c0 = P.dma("sync", n=7)(lambda sy: [
            sy.dma_start(out=ones_f[:], in_=onesf_in),
            sy.dma_start(out=ones_b[:], in_=onesb_in),
            sy.dma_start(out=id_bf[:], in_=idbf),
            sy.dma_start(out=mt_rot[:], in_=mtrot),
            sy.dma_start(out=lam_sb[:], in_=lamv),
            sy.dma_start(out=mo_col[0:1, :], in_=mo_in),
            sy.dma_start(out=rowa[0:1, :], in_=grep),
        ])

        t_eps = P.step("vector", waits=[])(lambda ve: [
            ve.memset(eps_c[:], EPS), ve.memset(t38_c[:], 1e-38)][-1])
        t_dots = P.step("tensor", waits=[t_c0])(lambda te: [
            te.matmul(pc[0:1, 0:1], lam_sb[:, 0:1], lam_sb[:, 1:2], start=True, stop=True),
            te.matmul(pc[0:1, 1:2], lam_sb[:, 2:3], lam_sb[:, 3:4], start=True, stop=True)][-1])
        t_exp = P.step("scalar", waits=[t_dots])(
            lambda ac: ac.activation(tinyr[0:1, 0:2], pc[0:1, 0:2], AF.Exp))
        t_lam = P.step("vector", waits=[t_exp])(lambda ve: [
            ve.tensor_sub(tinyr[0:1, 2:3], tinyr[0:1, 0:1], tinyr[0:1, 1:2]),
            ve.tensor_copy(tinyr[0:1, 8:9], tinyr[0:1, 8:9]),
            ve.tensor_scalar(tinyr[0:1, 3:4], tinyr[0:1, 2:3], 1.0, INIT_LAMBDA, ALU.mult, ALU.add)][-1])
        t_lbc = P.step("tensor", waits=[t_lam])(lambda te: [
            te.matmul(pc[:, 2:3], ones_f[0:1, :], tinyr[0:1, 3:4], start=True, stop=True),
            te.matmul(pc[:, 3:4], ones_f[0:1, :], mo_col[0:1, 0:1], start=True, stop=True)][-1])
        t_lev = P.step("vector", waits=[t_lbc])(lambda ve: [
            ve.tensor_copy(lam_col[:], pc[:, 2:3]),
            ve.tensor_copy(mo_col[:], pc[:, 3:4])][-1])

        r2c_jobs = [
            (grep, grep_c, 0), (brep, brep_c, 0), (ampb, ampb_c, 0), (phb, phb_c, 0),
            (l3g[0:1, :], lng_c, 0), (l4g[0:1, :], lng_c, 16),
            (l3g[1:2, :], lng_c, 32), (l4g[1:2, :], lng_c, 48),
            (l3b[0:1, :], lnb_c, 0), (l4b[0:1, :], lnb_c, 16),
            (l3b[1:2, :], lnb_c, 32), (l4b[1:2, :], lnb_c, 48),
        ]
        t_prev_tr = t_dots
        t_prev_ev = t_lev
        t_prev_ld = t_c0
        for src, dst, off in r2c_jobs:
            t_ld = P.dma("sync", waits=[t_prev_tr])(
                lambda sy, src=src: sy.dma_start(out=rowa[0:1, :], in_=src))
            t_tr = P.step("tensor", waits=[t_ld, t_prev_ev])(
                lambda te: [te.transpose(pc[:, j:j + 1], rowa[0:1, j * 128:(j + 1) * 128],
                                         ones_f[0:1, 0:1]) for j in range(16)][-1])
            t_ev = P.step("vector", waits=[t_tr])(
                lambda ve, dst=dst, off=off: ve.tensor_copy(dst[:, off:off + 16], pc[:, 0:16]))
            t_prev_tr, t_prev_ev = t_tr, t_ev

        P.flush(block)
        p0_done = P.barrier()
        p0.close()

        p1 = ExitStack()
        xn1 = sb(p1, "xn1", [128, 16 * 2048], BF16)
        strips = [sb(p1, "strips0", [128, 16 * 768], BF16),
                  sb(p1, "strips1", [128, 16 * 768], BF16)]
        stage = sb(p1, "stage", [128, 6 * 2048], BF16)
        xst = sb(p1, "xst", [128, 2 * 2048], F32)
        ropes = sb(p1, "ropes", [128, 2 * 1024], F32)
        qkraw = sb(p1, "qkraw", [128, 6 * 512], BF16)
        sqs = sb(p1, "sqs", [128, 2048], BF16)
        mm = [ps(p1, f"mm{i}", [128, 512], F32) for i in range(6)]
        trb = ps(p1, "trb", [128, 2048], BF16)
        rotv = trb[:].bitcast(F32)

        xn1v = xn1.rearrange("p (dc t) -> p dc t", t=2048)
        trbv = trb.rearrange("p (k o) -> p k o", o=128)

        stage_last_dma = {}
        strip_last_mm = {}
        rope_slot_last = {}
        rot_last = {}
        t_app_prev = {}
        t_sq_war = None
        t_trb_war = None
        bank_last_ev = {}
        last_epi = None
        bctr = 0
        for p_i in (0, 1):
            xsrc = x0 if p_i == 0 else x1
            t_tr_tile = {}
            t_ev_tile = {}
            for tile in range(16):
                slot = tile % 2
                w_ld = list(p0_done) if (p_i == 0 and tile < 2) else []
                if slot in t_app_prev:
                    w_ld.append(t_app_prev[slot])
                t_x = P.dma("sync", waits=w_ld)(
                    lambda sy, tile=tile, slot=slot, xsrc=xsrc:
                    sy.dma_start(out=xst[:, slot * 2048:(slot + 1) * 2048],
                                 in_=xsrc[tile * 128:(tile + 1) * 128, :]))
                xt = xst[:, slot * 2048:(slot + 1) * 2048]
                t_v1 = P.step("vector", waits=[t_x])(
                    lambda ve, xt=xt: ve.reduce_sum(lnst[:, 0:1], xt, axis=AX.X))
                w_sq = [t_x]
                if tile > 0:
                    w_sq.append(t_tr_tile[tile - 1])
                elif t_sq_war is not None:
                    w_sq.append(t_sq_war)
                t_a1 = P.step("scalar", waits=w_sq)(
                    lambda ac, xt=xt: ac.activation(sqs[:], xt, AF.Square, accum_out=lnst[:, 1:2]))
                t_v2 = P.step("vector", waits=[t_a1])(lambda ve: [
                    ve.tensor_scalar(lnst[:, 3:4], lnst[:, 1:2], 1.0 / 2048.0, None, ALU.mult),
                    ve.tensor_scalar(lnst[:, 2:3], lnst[:, 0:1], 1.0 / 2048.0, None, ALU.mult),
                    ve.tensor_copy(lnst[:, 0:1], lnst[:, 0:1]),
                    ve.tensor_mul(lnst[:, 4:5], lnst[:, 2:3], lnst[:, 2:3]),
                    ve.tensor_copy(lnst[:, 0:1], lnst[:, 0:1]),
                    ve.tensor_sub(lnst[:, 5:6], lnst[:, 3:4], lnst[:, 4:5])][-1])
                t_a2 = P.step("scalar", waits=[t_v2])(lambda ac: [
                    ac.activation(lnst[:, 6:7], lnst[:, 5:6], AF.Ln, bias=eps_c[:, 0:1]),
                    ac.activation(lnst[:, 6:7], lnst[:, 6:7], AF.Exp, scale=-0.5)][-1])
                t_v3 = P.step("vector", waits=[t_a2])(lambda ve, xt=xt: [
                    ve.tensor_scalar(lnst[:, 7:8], lnst[:, 2:3], lnst[:, 6:7], -1.0,
                                     ALU.mult, ALU.mult),
                    ve.tensor_copy(lnst[:, 0:1], lnst[:, 0:1]),
                    ve.tensor_scalar(sqs[:], xt, lnst[:, 6:7], lnst[:, 7:8],
                                     ALU.mult, ALU.add)][-1])
                if DEBUG and p_i == 0:
                    P.step("vector", waits=[t_v3])(
                        lambda ve, tile=tile: ve.tensor_copy(
                            dbgln_sb[:, tile * 8:(tile + 1) * 8], lnst[:]))
                t_app_prev[slot] = t_v3
                w_tr = [t_v3]
                if tile > 0:
                    w_tr.append(t_ev_tile[tile - 1])
                elif t_trb_war is not None:
                    w_tr.append(t_trb_war)
                t_t1 = P.step("tensor", waits=w_tr)(
                    lambda te: [te.transpose(trbv[:, dc, :], sqs[:, dc * 128:(dc + 1) * 128],
                                             id_bf[:]) for dc in range(16)][-1])
                t_tr_tile[tile] = t_t1
                t_v4 = P.step("vector", waits=[t_t1])(
                    lambda ve, tile=tile: ve.tensor_copy(
                        xn1v[:, :, tile * 128:(tile + 1) * 128], trbv[:]))
                t_ev_tile[tile] = t_v4
            t_sq_war = t_tr_tile[15]
            t_xn_done = [t_ev_tile[ts * 4 + 3] for ts in range(4)]
            if DEBUG and p_i == 0:
                tdmp = P.dma("gpsimd", waits=[t_ev_tile[15]])(
                    lambda g: g.dma_start(out=dbgxn, in_=xn1[:]))
                tdmp2 = P.dma("gpsimd", waits=[tdmp])(
                    lambda g: g.dma_start(out=dbgln, in_=dbgln_sb[:]))
            if TRUNC == "p1a" and p_i == 0:
                P.flush(block)

                @block.sync
                def _(sy, tdmp2=tdmp2):
                    sy.wait_ge(tdmp2[0].h, tdmp2[1])
                p1.close()
                return nc

            if p_i == 0:
                batches = [("q", 0, 6), ("q", 6, 6), ("q", 12, 4),
                           ("k", 16, 6), ("k", 22, 6), ("k", 28, 4),
                           ("v", 32, 6), ("v", 38, 6), ("v", 44, 4)]
            else:
                batches = [("k", 16, 6), ("k", 22, 6), ("k", 28, 4),
                           ("v", 32, 6), ("v", 38, 6), ("v", 44, 4)]
            for kind, jc0, n_jc in batches:
                ssl = bctr % 2
                bctr += 1
                w_s = [tok_cast_wqkv]
                if ssl in strip_last_mm:
                    w_s.append(strip_last_mm[ssl])
                t_strip = P.dma("scalar", waits=w_s, n=16)(
                    lambda sc, jc0=jc0, n_jc=n_jc, ssl=ssl: [
                        sc.dma_start_transpose(
                            strips[ssl][:, dc * 768:dc * 768 + n_jc * 128],
                            wqkv_bf[jc0 * 128:(jc0 + n_jc) * 128, dc * 128:(dc + 1) * 128])
                        for dc in range(16)])
                strip = strips[ssl]
                halves = [list(range(0, min(3, n_jc))), list(range(3, n_jc))]
                t_evT = {}
                t_epi = {}
                for ts in range(4):
                    if kind in ("q", "k"):
                        c0 = 0 if kind == "q" else 2
                        rsl = ts % 2
                        w_r = []
                        if rsl in rope_slot_last:
                            w_r.append(rope_slot_last[rsl])
                        t_rsl = P.dma("sync", waits=w_r)(
                            lambda sy, c0=c0, ts=ts, p_i=p_i, rsl=rsl:
                            sy.dma_start(out=ropes[:, rsl * 1024:(rsl + 1) * 1024]
                                         .rearrange("p (c t) -> p c t", t=512),
                                         in_=ropetab[c0:c0 + 2, :,
                                                     p_i * 2048 + ts * 512:p_i * 2048 + (ts + 1) * 512]
                                         .rearrange("c p t -> p c t")))
                    for hi, half in enumerate(halves):
                        if not half:
                            continue
                        w_m = [t_strip, t_xn_done[ts]]
                        if ts > 0:
                            w_m.append(t_evT[(ts - 1, hi)])
                        elif hi in bank_last_ev:
                            w_m.append(bank_last_ev[hi])
                        t_mmh = P.step("tensor", waits=w_m)(
                            lambda te, half=half, strip=strip, ts=ts: [
                                te.matmul(mm[jcb][:],
                                          strip[:, dc * 768 + jcb * 128:dc * 768 + (jcb + 1) * 128],
                                          xn1v[:, dc, ts * 512:(ts + 1) * 512],
                                          start=(dc == 0), stop=(dc == 15))
                                for dc in range(16) for jcb in half][-1])
                        strip_last_mm[ssl] = t_mmh
                        ev_last = None
                        for jcb in half:
                            if kind == "v":
                                w_e = [t_mmh]
                                if jcb in stage_last_dma:
                                    w_e.append(stage_last_dma[jcb])
                                tok = P.step("vector", waits=w_e)(
                                    lambda ve, jcb=jcb, ts=ts: ve.tensor_copy(
                                        stage[:, jcb * 2048 + ts * 512:jcb * 2048 + (ts + 1) * 512],
                                        mm[jcb][:]))
                                ev_last = tok
                                t_epi[(jcb, ts)] = tok
                                last_epi = tok
                            else:
                                t_eq = P.step("vector", waits=[t_mmh])(
                                    lambda ve, jcb=jcb: ve.tensor_copy(
                                        qkraw[:, jcb * 512:(jcb + 1) * 512], mm[jcb][:]))
                                ev_last = t_eq
                                rsl2 = jcb % 2
                                w_rot = [t_eq]
                                if rsl2 in rot_last:
                                    w_rot.append(rot_last[rsl2])
                                t_rot = P.step("tensor", waits=w_rot)(
                                    lambda te, jcb=jcb, rsl2=rsl2: te.matmul(
                                        rotv[:, rsl2 * 512:(rsl2 + 1) * 512],
                                        mt_rot[:], qkraw[:, jcb * 512:(jcb + 1) * 512],
                                        start=True, stop=True))
                                w_rp = [t_rot, t_rsl]
                                if jcb in stage_last_dma:
                                    w_rp.append(stage_last_dma[jcb])
                                tok = P.step("vector", waits=w_rp)(
                                    lambda ve, jcb=jcb, ts=ts, rsl=rsl, rsl2=rsl2: [
                                        ve.tensor_mul(sct[:], qkraw[:, jcb * 512:(jcb + 1) * 512],
                                                      ropes[:, rsl * 1024:rsl * 1024 + 512]),
                                        ve.tensor_mul(sct2[:],
                                                      rotv[:, rsl2 * 512:(rsl2 + 1) * 512],
                                                      ropes[:, rsl * 1024 + 512:rsl * 1024 + 1024]),
                                        ve.tensor_add(
                                            stage[:, jcb * 2048 + ts * 512:jcb * 2048 + (ts + 1) * 512],
                                            sct[:], sct2[:])][-1])
                                rot_last[rsl2] = tok
                                rope_slot_last[rsl] = tok
                                t_epi[(jcb, ts)] = tok
                                last_epi = tok
                        t_evT[(ts, hi)] = ev_last
                for hi in (0, 1):
                    if (3, hi) in t_evT:
                        bank_last_ev[hi] = t_evT[(3, hi)]
                for jcb in range(n_jc):
                    jc = jc0 + jcb
                    if kind == "q":
                        dst = qscr[jc, :, 0:2048]
                    elif kind == "k":
                        dst = kscr[jc - 16, :, p_i * 2048:(p_i + 1) * 2048]
                    else:
                        dst = vscr[jc - 32, :, p_i * 2048:(p_i + 1) * 2048]
                    stage_last_dma[jcb] = P.dma("gpsimd", waits=[t_epi[(jcb, 3)]])(
                        lambda g, jcb=jcb, dst=dst: g.dma_start(
                            out=dst, in_=stage[:, jcb * 2048:(jcb + 1) * 2048]))
            t_trb_war = last_epi
            P.flush(block)

        p1_done = P.barrier()
        p1.close()

        p2 = ExitStack()
        pt1 = sb(p2, "pt1", [128, 16384], BF16)
        pt2 = sb(p2, "pt2", [128, 16384], BF16)
        qh = sb(p2, "qh", [128, 2 * 2048], BF16)
        kh = sb(p2, "kh", [128, 2 * 4096], BF16)
        vfm = sb(p2, "vfm", [128, 2 * 4096], BF16)
        vtm = sb(p2, "vtm", [128, 32 * 129], BF16)
        qsq = sb(p2, "qsq", [128, 2048], BF16)
        ksq = sb(p2, "ksq", [128, 4096], BF16)
        obuf = sb(p2, "obuf", [128, 2 * 4 * 129], F32)
        rsb = sb(p2, "rsb", [128, 8], F32)
        atst = sb(p2, "atst", [128, 2 * 512], BF16)
        mband = sb(p2, "mband", [128, 4 * 512], BF16)
        stp = ps(p2, "stp", [128, 2048], F32)
        op1 = ps(p2, "op1", [128, 512], F32)
        op2 = ps(p2, "op2", [128, 512], F32)
        trb2 = ps(p2, "trb2", [128, 1024], BF16)
        aux2 = ps(p2, "aux2", [128, 512], F32)
        trb2v = trb2.rearrange("p (k o) -> p k o", o=128)
        vtmv = vtm.rearrange("p (k o) -> p k o", o=129)

        t_mbl = P.dma("sync", waits=p1_done)(
            lambda sy: sy.dma_start(out=mband.rearrange("p (r q) -> p r q", q=512),
                                    in_=mbands.rearrange("r p q -> p r q")))
        t_vones = P.step("vector", waits=p1_done)(
            lambda ve: ve.memset(vtmv[:, :, 128:129], 1.0))

        t_exp_last = {1: None, 2: None}
        t_pv_last = {1: None, 2: None}
        t_oev_last = {1: None, 2: None}
        t_comb_q = {}
        t_load = {}
        t_vte = {}
        hs = None
        for h in range(H_):
            sl = h % 2
            w_l = list(p1_done) if h == 0 else []
            if h >= 2:
                w_l = [t_pv_last[2], t_knB]
            t_load[h] = P.dma("sync", waits=w_l, n=3)(lambda sy, h=h, sl=sl: [
                sy.dma_start(out=qh[:, sl * 2048:(sl + 1) * 2048], in_=qscr[h]),
                sy.dma_start(out=kh[:, sl * 4096:(sl + 1) * 4096], in_=kscr[h]),
                sy.dma_start(out=vfm[:, sl * 4096:(sl + 1) * 4096], in_=vscr[h])])
            t_vt_ev_prev = t_vones if h == 0 else t_vte[(h - 1, 3)]
            w_vt0 = [t_load[h]]
            if h > 0:
                w_vt0.append(t_pv_last[2])
            for g in range(4):
                w_vt = list(w_vt0) if g == 0 else [t_vte[(h, g - 1)]]
                t_vt = P.step("tensor", waits=w_vt)(
                    lambda te, g=g, sl=sl: [te.transpose(
                        trb2v[:, j, :], vfm[:, sl * 4096 + (g * 8 + j) * 128: sl * 4096 + (g * 8 + j + 1) * 128],
                        id_bf[:]) for j in range(8)][-1])
                t_vte[(h, g)] = P.step("vector", waits=[t_vt, t_vones])(
                    lambda ve, g=g: ve.tensor_copy(vtmv[:, g * 8:(g + 1) * 8, 0:128], trb2v[:]))
            w_sq2 = [t_load[h]]
            if t_exp_last[2] is not None:
                w_sq2.append(t_exp_last[2])
            t_qsq = P.step("scalar", waits=w_sq2)(
                lambda ac, sl=sl: ac.activation(qsq[:], qh[:, sl * 2048:(sl + 1) * 2048], AF.Square))
            t_ksq = P.step("scalar", waits=[t_load[h]])(
                lambda ac, sl=sl: ac.activation(ksq[:], kh[:, sl * 4096:(sl + 1) * 4096], AF.Square))
            t_redprev = None
            for i in (1, 2):
                base = (i - 1) * 64
                tp = (base, 0)
                w_qn = [t_qsq]
                if t_exp_last[2] is not None:
                    w_qn.append(t_exp_last[2])
                if t_redprev is not None:
                    w_qn.append(t_redprev)
                t_qn = P.step("tensor", waits=w_qn)(
                    lambda te, base=base, tp=tp: [te.matmul(
                        stp[0:1, c * 512:(c + 1) * 512], ones_b[base:base + 64, 0:1],
                        qsq[base:base + 64, c * 512:(c + 1) * 512], start=True, stop=True,
                        tile_position=tp) for c in range(4)][-1])
                t_qr = P.step("vector", waits=[t_qn])(
                    lambda ve, i=i: ve.reduce_max(tinyr[0:1, 10 + i:11 + i], stp[0:1, 0:2048], axis=AX.X))
                t_k1 = P.step("tensor", waits=[t_ksq, t_qr])(
                    lambda te, base=base, tp=tp, sl=sl: [te.matmul(
                        stp[0:1, c * 512:(c + 1) * 512], ones_b[base:base + 64, 0:1],
                        ksq[base:base + 64, c * 512:(c + 1) * 512], start=True, stop=True,
                        tile_position=tp) for c in range(4)][-1])
                t_k1r = P.step("vector", waits=[t_k1])(
                    lambda ve: ve.reduce_max(tinyr[0:1, 20:21], stp[0:1, 0:2048], axis=AX.X))
                t_k2 = P.step("tensor", waits=[t_k1r])(
                    lambda te, base=base, tp=tp: [te.matmul(
                        stp[0:1, c * 512:(c + 1) * 512], ones_b[base:base + 64, 0:1],
                        ksq[base:base + 64, (4 + c) * 512:(5 + c) * 512], start=True, stop=True,
                        tile_position=tp) for c in range(4)][-1])
                t_k2r = P.step("vector", waits=[t_k2])(lambda ve, i=i: [
                    ve.reduce_max(tinyr[0:1, 21:22], stp[0:1, 0:2048], axis=AX.X),
                    ve.tensor_copy(lnst[:, 1:2], lnst[:, 1:2]),
                    ve.tensor_max(tinyr[0:1, 22:23], tinyr[0:1, 20:21], tinyr[0:1, 21:22]),
                    ve.tensor_copy(lnst[:, 1:2], lnst[:, 1:2]),
                    ve.tensor_mul(tinyr[0:1, 24 + i:25 + i], tinyr[0:1, 10 + i:11 + i],
                                  tinyr[0:1, 22:23])][-1])
                t_redprev = t_k2r
            t_B = P.step("scalar", waits=[t_redprev])(lambda ac: [
                ac.activation(tinyr[0:1, 30:32], tinyr[0:1, 25:27], AF.Ln, bias=t38_c[0:1, 0:1]),
                ac.activation(tinyr[0:1, 30:32], tinyr[0:1, 30:32], AF.Exp, scale=0.5)][-1])
            t_Bn = P.step("vector", waits=[t_B])(
                lambda ve: ve.tensor_scalar(tinyr[0:1, 32:34], tinyr[0:1, 30:32], -1.0, None, ALU.mult))
            t_Bb = P.step("tensor", waits=[t_Bn])(
                lambda te: te.matmul(aux2[:, 0:2], ones_f[0:1, :], tinyr[0:1, 32:34],
                                     start=True, stop=True))
            t_knB = P.step("vector", waits=[t_Bb])(lambda ve: [
                ve.tensor_copy(nbias[:, 0:1], aux2[:, 0:1]),
                ve.tensor_copy(nbias[:, 4:5], aux2[:, 1:2]),
                ve.tensor_scalar(nbias[:, 1:2], nbias[:, 0:1], 1.0, None, ALU.add),
                ve.tensor_scalar(nbias[:, 5:6], nbias[:, 4:5], 1.0, None, ALU.add),
                ve.tensor_scalar(nbias[:, 2:3], nbias[:, 0:1], mo_col[:], None, ALU.add),
                ve.tensor_scalar(nbias[:, 6:7], nbias[:, 4:5], mo_col[:], None, ALU.add)][-1])

            for qs in range(4):
                t_madd = {}
                t_exp_tok = {1: [], 2: []}
                for kcp in range(16):
                    kc = 2 * kcp
                    cls = kcls(kc, qs)
                    for i in (1, 2):
                        pt = pt1 if i == 1 else pt2
                        base = (i - 1) * 64
                        scol = (i - 1) * 1024
                        w_s = [t_load[h], t_knB]
                        if kcp > 0:
                            w_s.append(t_exp_tok[i][-1])
                        elif t_pv_last[i] is not None and qs > 0:
                            w_s.append(t_pv_last[i])
                        t_s = P.step("tensor", waits=w_s)(
                            lambda te, kc=kc, base=base, scol=scol, sl=sl, qs=qs: [
                                te.matmul(stp[:, scol + j * 512:scol + (j + 1) * 512],
                                          kh[base:base + 64,
                                             sl * 4096 + (kc + j) * 128:sl * 4096 + (kc + j + 1) * 128],
                                          qh[base:base + 64, sl * 2048 + qs * 512:sl * 2048 + (qs + 1) * 512],
                                          start=True, stop=True, tile_position=(base, 0))
                                for j in range(2)][-1])
                        w_e = [t_s]
                        if cls == 3:
                            r = kc - qs * 4
                            t_ma = P.step("vector", waits=[t_s, t_mbl])(
                                lambda ve, r=r, scol=scol: [
                                    ve.tensor_add(stp[:, scol:scol + 512], stp[:, scol:scol + 512],
                                                  mband[:, r * 512:(r + 1) * 512]),
                                    ve.tensor_add(stp[:, scol + 512:scol + 1024],
                                                  stp[:, scol + 512:scol + 1024],
                                                  mband[:, (r + 1) * 512:(r + 2) * 512])][-1])
                            w_e = [t_ma]
                        bcol = (i - 1) * 4 + (0 if cls == 3 else cls)
                        w_exp = list(w_e)
                        if kcp == 0 and t_pv_last[i] is not None:
                            w_exp.append(t_pv_last[i])
                        t_e = P.step("scalar", waits=w_exp)(
                            lambda ac, pt=pt, kc=kc, scol=scol, bcol=bcol, qs=qs:
                            ac.activation(pt[:, kc * 512:(kc + 2) * 512],
                                          stp[:, scol:scol + 1024], AF.Exp,
                                          bias=nbias[:, bcol:bcol + 1]))
                        t_exp_tok[i].append(t_e)
                t_exp_last[1] = t_exp_tok[1][-1]
                t_exp_last[2] = t_exp_tok[2][-1]
                t_oev = {}
                for i in (1, 2):
                    pt = pt1 if i == 1 else pt2
                    op = op1 if i == 1 else op2
                    for qt in range(4):
                        w_pv = [t_exp_tok[i][-1], t_vte[(h, 3)]]
                        if (i, qt - 1) in t_oev:
                            w_pv.append(t_oev[(i, qt - 1)])
                        elif t_oev_last[i] is not None:
                            w_pv.append(t_oev_last[i])
                        t_pv = P.step("tensor", waits=w_pv)(
                            lambda te, pt=pt, op=op, qt=qt: [
                                te.matmul(op[:, 0:129],
                                          pt[:, kc * 512 + qt * 128:kc * 512 + (qt + 1) * 128],
                                          vtm[:, kc * 129:(kc + 1) * 129],
                                          start=(kc == 0), stop=(kc == 31))
                                for kc in range(32)][-1])
                        t_pv_last[i] = t_pv
                        t_oev[(i, qt)] = t_oev_last[i] = P.step("vector", waits=[t_pv])(
                            lambda ve, i=i, qt=qt, op=op: ve.tensor_copy(
                                obuf[:, (i - 1) * 516 + qt * 129:(i - 1) * 516 + (qt + 1) * 129],
                                op[:, 0:129]))
                o1 = obuf[:, 0:516].rearrange("p (s c) -> p s c", c=129)
                o2 = obuf[:, 516:1032].rearrange("p (s c) -> p s c", c=129)
                asl = qs % 2
                w_c = [t_oev[(1, 3)], t_oev[(2, 3)], t_lev]
                if (h, qs - 2) in t_comb_q or (h - 1, qs + 2) in t_comb_q:
                    kk = (h, qs - 2) if (h, qs - 2) in t_comb_q else (h - 1, qs + 2)
                    w_c.append(t_comb_q[kk])
                sct3 = sct.rearrange("p (s c) -> p s c", c=128)
                sct23 = sct2.rearrange("p (s c) -> p s c", c=128)
                t_cmb = P.step("vector", waits=w_c)(lambda ve, o1=o1, o2=o2, asl=asl,
                                                    sct3=sct3, sct23=sct23: [
                    ve.reciprocal(rsb[:, 0:4], o1[:, :, 128]),
                    ve.reciprocal(rsb[:, 4:8], o2[:, :, 128]),
                    ve.tensor_mul(sct3, o1[:, :, 0:128],
                                  rsb[:, 0:4].rearrange("p (s o) -> p s o", o=1)
                                  .broadcast_to([128, 4, 128])),
                    ve.tensor_mul(sct23, o2[:, :, 0:128],
                                  rsb[:, 4:8].rearrange("p (s o) -> p s o", o=1)
                                  .broadcast_to([128, 4, 128])),
                    ve.tensor_scalar(sct2[:], sct2[:], lam_col[:], None, ALU.mult),
                    ve.tensor_sub(atst[:, asl * 512:(asl + 1) * 512], sct[:], sct2[:])][-1])
                tok_a = P.dma("gpsimd", waits=[t_cmb])(
                    lambda g, h=h, qs=qs, asl=asl: g.dma_start(
                        out=attnscr[qs * 512:(qs + 1) * 512, h * 128:(h + 1) * 128]
                        .rearrange("(s p) c -> p s c", p=128),
                        in_=atst[:, asl * 512:(asl + 1) * 512]
                        .rearrange("p (s c) -> p s c", c=128)))
                t_comb_q[(h, qs)] = tok_a
            P.flush(block)

        p2_done = P.barrier()
        p2.close()

        p3 = ExitStack()
        wost = sb(p3, "wost", [128, 16 * 2048], BF16)
        atile = sb(p3, "atile", [128, 2 * 2048], BF16)
        gnb = sb(p3, "gnb", [128, 2 * 2048], BF16)
        gnfm = sb(p3, "gnfm", [128, 16 * 512], BF16)
        r1a = sb(p3, "r1a", [128, 4 * 2048], F32)
        wob = sb(p3, "wob", [128, 512], BF16)
        sq3 = sb(p3, "sq3", [128, 2048], BF16)
        mm3 = [ps(p3, f"mm3_{i}", [128, 512], F32) for i in range(6)]
        trb3 = ps(p3, "trb3", [128, 1024], BF16)
        aux3 = ps(p3, "aux3", [128, 512], F32)
        trb3v = trb3.rearrange("p (k o) -> p k o", o=128)
        gnfmv = gnfm.rearrange("p (dc t) -> p dc t", t=512)

        t_wost = P.dma("scalar", waits=[tok_cast_wo] + p2_done, n=16)(
            lambda sc: [sc.dma_start_transpose(
                wost[:, dc * 2048:(dc + 1) * 2048],
                wo_bf[:, dc * 128:(dc + 1) * 128]) for dc in range(16)])

        t_gtr = {}
        t_gev = {}
        t_wadd = {}
        t_r1w = {}
        t_wtr_prev = None
        t_wadd_prev = None
        for grp in range(4):
            w_x = list(p2_done) if grp == 0 else [t_r1w[grp - 1]]
            t_xr = P.dma("sync", waits=w_x, n=4)(
                lambda sy, grp=grp: [sy.dma_start(
                    out=r1a[:, tsub * 2048:(tsub + 1) * 2048],
                    in_=x0[grp * 512 + tsub * 128:grp * 512 + (tsub + 1) * 128, :])
                    for tsub in range(4)])
            for pair in range(2):
                t_at = {}
                t_stats = {}
                for t2 in range(2):
                    tile = grp * 4 + pair * 2 + t2
                    slot = tile % 2
                    w_a = list(p2_done) if tile < 2 else [t_gtr[tile - 2]]
                    t_at[t2] = P.dma("sync", waits=w_a)(
                        lambda sy, tile=tile, slot=slot: sy.dma_start(
                            out=atile[:, slot * 2048:(slot + 1) * 2048],
                            in_=attnscr[tile * 128:(tile + 1) * 128, :]))
                    at = atile[:, slot * 2048:(slot + 1) * 2048]
                    t_s1 = P.step("vector", waits=[t_at[t2]])(
                        lambda ve, at=at, t2=t2: ve.reduce_sum(stat4[:, t2:t2 + 1], at, axis=AX.X))
                    t_s2 = P.step("scalar", waits=[t_at[t2], t_s1])(
                        lambda ac, at=at, t2=t2: ac.activation(
                            sq3[:], at, AF.Square, accum_out=stat4[:, 2 + t2:3 + t2]))
                    t_stats[t2] = t_s2
                t_cmb3 = P.step("tensor", waits=[t_stats[0], t_stats[1]])(
                    lambda te: te.matmul(aux3[0:1, 0:4], ones_f[:, 0:1], stat4[:, 0:4],
                                         start=True, stop=True))
                t_gs = P.step("vector", waits=[t_cmb3])(lambda ve: [
                    ve.tensor_copy(tinyr[0:1, 36:40], aux3[0:1, 0:4]),
                    ve.tensor_copy(lnst[:, 1:2], lnst[:, 1:2]),
                    ve.tensor_add(tinyr[0:1, 40:41], tinyr[0:1, 36:37], tinyr[0:1, 37:38]),
                    ve.tensor_add(tinyr[0:1, 41:42], tinyr[0:1, 38:39], tinyr[0:1, 39:40]),
                    ve.tensor_scalar(tinyr[0:1, 42:43], tinyr[0:1, 40:41], 1.0 / 524288.0, None, ALU.mult),
                    ve.tensor_scalar(tinyr[0:1, 43:44], tinyr[0:1, 41:42], 1.0 / 524288.0, None, ALU.mult),
                    ve.tensor_mul(tinyr[0:1, 44:45], tinyr[0:1, 42:43], tinyr[0:1, 42:43]),
                    ve.tensor_copy(lnst[:, 1:2], lnst[:, 1:2]),
                    ve.tensor_sub(tinyr[0:1, 45:46], tinyr[0:1, 43:44], tinyr[0:1, 44:45])][-1])
                t_gr = P.step("scalar", waits=[t_gs])(lambda ac: [
                    ac.activation(tinyr[0:1, 46:47], tinyr[0:1, 45:46], AF.Ln, bias=eps_c[0:1, 0:1]),
                    ac.activation(tinyr[0:1, 46:47], tinyr[0:1, 46:47], AF.Exp, scale=-0.5)][-1])
                t_gb = P.step("tensor", waits=[t_gr])(
                    lambda te: te.matmul(aux3[:, 8:10], ones_f[0:1, :], tinyr[0:1, 42:43]
                                         if False else tinyr[0:1, 42:44].rearrange("a b -> a b"),
                                         start=True, stop=True))
                t_gb2 = P.step("vector", waits=[t_gb])(lambda ve: [
                    ve.tensor_copy(mcol[:], aux3[:, 8:9])][-1])
                t_gb3 = P.step("tensor", waits=[t_gb2, t_gr])(
                    lambda te: te.matmul(aux3[:, 10:11], ones_f[0:1, :], tinyr[0:1, 46:47],
                                         start=True, stop=True))
                t_gb4 = P.step("vector", waits=[t_gb3])(
                    lambda ve: ve.tensor_copy(rcol[:], aux3[:, 10:11]))
                for t2 in range(2):
                    tile = grp * 4 + pair * 2 + t2
                    slot = tile % 2
                    at = atile[:, slot * 2048:(slot + 1) * 2048]
                    t_sc = P.step("vector", waits=[t_gb4])(lambda ve, tile=tile: [
                        ve.tensor_copy(lnst[:, 1:2], lnst[:, 1:2]),
                        ve.tensor_scalar(sccol[:, 0:1], grep_c[:, tile:tile + 1], rcol[:], 0.2,
                                         ALU.mult, ALU.mult),
                        ve.tensor_scalar(lnst[:, 0:1], brep_c[:, tile:tile + 1], 0.2, None, ALU.mult),
                        ve.tensor_mul(sccol[:, 1:2], sccol[:, 0:1], mcol[:]),
                        ve.tensor_copy(lnst[:, 2:3], lnst[:, 2:3]),
                        ve.tensor_sub(sccol[:, 1:2], lnst[:, 0:1], sccol[:, 1:2]),
                        ve.tensor_copy(lnst[:, 1:2], lnst[:, 1:2])][-1])
                    w_app = [t_sc]
                    if tile >= 2:
                        w_app.append(t_gtr[tile - 2])
                    t_app = P.step("vector", waits=w_app)(
                        lambda ve, at=at, slot=slot: ve.tensor_scalar(
                            gnb[:, slot * 2048:(slot + 1) * 2048], at, sccol[:, 0:1],
                            sccol[:, 1:2], ALU.mult, ALU.add))
                    w_tr = [t_app]
                    if tile > 0:
                        w_tr.append(t_gev[tile - 1])
                    if tile % 4 == 0 and grp > 0:
                        w_tr.append(t_wadd[(grp - 1, 15)])
                    t_tr3a = P.step("tensor", waits=w_tr)(
                        lambda te, slot=slot: [te.transpose(
                            trb3v[:, j, :], gnb[:, slot * 2048 + j * 128:slot * 2048 + (j + 1) * 128],
                            id_bf[:]) for j in range(8)][-1])
                    t_ev3a = P.step("vector", waits=[t_tr3a])(
                        lambda ve, tile=tile: ve.tensor_copy(
                            gnfmv[:, 0:8, (tile % 4) * 128:(tile % 4 + 1) * 128], trb3v[:]))
                    t_tr3b = P.step("tensor", waits=[t_ev3a])(
                        lambda te, slot=slot: [te.transpose(
                            trb3v[:, j - 8, :], gnb[:, slot * 2048 + j * 128:slot * 2048 + (j + 1) * 128],
                            id_bf[:]) for j in range(8, 16)][-1])
                    t_gtr[tile] = t_tr3b
                    t_gev[tile] = P.step("vector", waits=[t_tr3b])(
                        lambda ve, tile=tile: ve.tensor_copy(
                            gnfmv[:, 8:16, (tile % 4) * 128:(tile % 4 + 1) * 128], trb3v[:]))
            jgrps = [list(range(0, 6)), list(range(6, 12)), list(range(12, 16))]
            for jg in jgrps:
                for bi_, jc in enumerate(jg):
                    w_m = [t_wost, t_gev[grp * 4 + 3], t_xr]
                    if jc >= 6:
                        w_m.append(t_wadd[(grp, jc - 6)])
                    elif grp > 0:
                        w_m.append(t_wadd[(grp - 1, 12 + jc if jc < 4 else 6 + jc)])
                    t_wm = P.step("tensor", waits=w_m)(
                        lambda te, jc=jc: [te.matmul(
                            mm3[jc % 6][:], wost[:, dc * 2048 + jc * 128:dc * 2048 + (jc + 1) * 128],
                            gnfmv[:, dc, :], start=(dc == 0), stop=(dc == 15))
                            for dc in range(16)][-1])
                    w_we = [t_wm]
                    if t_wtr_prev is not None:
                        w_we.append(t_wtr_prev)
                    t_wev = P.step("vector", waits=w_we)(
                        lambda ve, jc=jc: ve.tensor_copy(wob[:], mm3[jc % 6][:]))
                    t_wtr = P.step("tensor", waits=[t_wev, t_wadd_prev] if t_wadd_prev
                                   else [t_wev])(
                        lambda te: [te.transpose(trb3v[:, 4 + tsub, :],
                                                 wob[:, tsub * 128:(tsub + 1) * 128], id_bf[:])
                                    for tsub in range(4)][-1])
                    t_wtr_prev = t_wtr
                    t_wa = P.step("vector", waits=[t_wtr])(
                        lambda ve, jc=jc: ve.tensor_add(
                            r1a.rearrange("p (s c) -> p s c", c=2048)[:, :, jc * 128:(jc + 1) * 128],
                            r1a.rearrange("p (s c) -> p s c", c=2048)[:, :, jc * 128:(jc + 1) * 128],
                            trb3v[:, 4:8, :]))
                    t_wadd[(grp, jc)] = t_wa
                    t_wadd_prev = t_wa
            t_r1w[grp] = P.dma("gpsimd", waits=[t_wadd[(grp, j)] for j in range(16)])(
                lambda g, grp=grp: g.dma_start(
                    out=r1scr[grp * 512:(grp + 1) * 512, :].rearrange("(s p) c -> p s c", p=128),
                    in_=r1a.rearrange("p (s c) -> p s c", c=2048)))
            P.flush(block)

        p3_done = P.barrier()
        p3.close()

        p4 = ExitStack()
        cacc = sb(p4, "cacc", [128, 16 * 1024], F32)
        astr = [sb(p4, "astr0", [128, 8 * 2064], BF16),
                sb(p4, "astr1", [128, 8 * 2064], BF16)]
        bbuf = [sb(p4, "bbuf0", [128, 8 * 1024], BF16),
                sb(p4, "bbuf1", [128, 8 * 1024], BF16)]
        bacc = sb(p4, "bacc", [1, 2048], F32)
        cst = sb(p4, "cst", [128, 1024], BF16)
        rowb = cacc[0:1, 7168:9216]
        rowc = cacc[0:1, 9216:11264]
        modbc = cacc[:, 0:4096]
        pwb = cacc[:, 4096:6144].bitcast(BF16)
        wef = cacc[:, 6144:7168].bitcast(BF16)
        fp = [ps(p4, f"fp{i}", [128, 512], F32) for i in range(8)]

        t_fev = {}
        t_badd = {}
        t_cw = {}
        prev_cw_tokens = list(p3_done)
        for l in range(2):
            for half in range(2):
                for kcg in range(8):
                    sl = kcg % 2
                    w_a = [tok_cast_a]
                    w_b = [tok_cast_a]
                    key2 = (l, half, kcg - 2)
                    if kcg >= 2 and key2 in t_badd:
                        w_a.append(t_badd[key2]); w_b.append(t_badd[key2])
                    elif half > 0 or l > 0:
                        pk = (l, half - 1, 6 + sl) if half > 0 else (l - 1, 1, 6 + sl)
                        w_a.append(t_badd[pk]); w_b.append(t_badd[pk])
                    elif l == 0 and half == 0 and kcg < 2:
                        w_a += prev_cw_tokens; w_b += prev_cw_tokens
                    t_as = P.dma("scalar", waits=w_a, n=8)(
                        lambda sc, l=l, kcg=kcg, sl=sl: [sc.dma_start_transpose(
                            astr[sl][:, k * 2064:(k + 1) * 2064],
                            a_bf[l, :, (kcg * 8 + k) * 128:(kcg * 8 + k + 1) * 128])
                            for k in range(8)])
                    t_bl = P.dma("gpsimd", waits=w_b)(
                        lambda g, l=l, half=half, kcg=kcg, sl=sl: g.dma_start(
                            out=bbuf[sl].rearrange("p (k j) -> p k j", j=1024),
                            in_=entpw[l, kcg * 1024:(kcg + 1) * 1024,
                                      half * 1024:(half + 1) * 1024]
                            .rearrange("(k p) j -> p k j", p=128)))
                    w_f = [t_as, t_bl]
                    key1 = (l, half, kcg - 1)
                    if key1 in t_badd:
                        w_f.append(t_badd[key1])
                    elif half > 0 or l > 0:
                        pk = (l, half - 1, 7) if half > 0 else (l - 1, 1, 7)
                        w_f.append(t_badd[pk])
                    t_fm = P.step("tensor", waits=w_f)(
                        lambda te, sl=sl: [te.matmul(
                            fp[ic // 2][:, (ic % 2) * 256:(ic % 2 + 1) * 256],
                            astr[sl][:, k * 2064 + ic * 128:k * 2064 + (ic + 1) * 128],
                            bbuf[sl][:, k * 1024 + jb * 256:k * 1024 + (jb + 1) * 256],
                            start=(k == 0 and ic % 2 == 0), stop=(k == 7 and ic % 2 == 1))
                            for jb in range(1) for k in range(8) for ic in range(16)][-1])
                    t_fev_l = []
                    t_f_last = t_fm
                    for jb in range(4):
                        if jb > 0:
                            t_f_last = P.step("tensor", waits=[t_fev_l[-1]])(
                                lambda te, sl=sl, jb=jb: [te.matmul(
                                    fp[ic // 2][:, (ic % 2) * 256:(ic % 2 + 1) * 256],
                                    astr[sl][:, k * 2064 + ic * 128:k * 2064 + (ic + 1) * 128],
                                    bbuf[sl][:, k * 1024 + jb * 256:k * 1024 + (jb + 1) * 256],
                                    start=(k == 0 and ic % 2 == 0), stop=(k == 7 and ic % 2 == 1))
                                    for k in range(8) for ic in range(16)][-1])
                        t_fe = P.step("vector", waits=[t_f_last])(
                            lambda ve, kcg=kcg, jb=jb: [
                                (ve.tensor_copy(cacc[:, ic * 1024 + jb * 256:ic * 1024 + (jb + 1) * 256],
                                                fp[ic // 2][:, (ic % 2) * 256:(ic % 2 + 1) * 256])
                                 if kcg == 0 else
                                 ve.scalar_tensor_tensor(
                                     cacc[:, ic * 1024 + jb * 256:ic * 1024 + (jb + 1) * 256],
                                     fp[ic // 2][:, (ic % 2) * 256:(ic % 2 + 1) * 256], 1.0,
                                     cacc[:, ic * 1024 + jb * 256:ic * 1024 + (jb + 1) * 256],
                                     ALU.mult, ALU.add))
                                for ic in range(16)][-1])
                        t_fev_l.append(t_fe)
                    t_bm = P.step("tensor", waits=[t_fev_l[-1]])(
                        lambda te, sl=sl: [te.matmul(
                            fp[jb][0:1, 0:256],
                            astr[sl][:, k * 2064 + 2048:k * 2064 + 2049],
                            bbuf[sl][:, k * 1024 + jb * 256:k * 1024 + (jb + 1) * 256],
                            start=(k == 0), stop=(k == 7))
                            for jb in range(4) for k in range(8)][-1])
                    t_ba = P.step("vector", waits=[t_bm])(
                        lambda ve, l=l, half=half, kcg=kcg: [
                            (ve.tensor_copy(bacc[0:1, half * 1024 + jb * 256:
                                                 half * 1024 + (jb + 1) * 256],
                                            fp[jb][0:1, 0:256])
                             if kcg == 0 else
                             ve.scalar_tensor_tensor(
                                 bacc[0:1, half * 1024 + jb * 256:half * 1024 + (jb + 1) * 256],
                                 fp[jb][0:1, 0:256], 1.0,
                                 bacc[0:1, half * 1024 + jb * 256:half * 1024 + (jb + 1) * 256],
                                 ALU.mult, ALU.add))
                            for jb in range(4)][-1])
                    t_badd[(l, half, kcg)] = t_ba
                for ic in range(16):
                    t_ce = P.step("vector", waits=[t_badd[(l, half, 7)]])(
                        lambda ve, ic=ic: ve.tensor_copy(cst[:], cacc[:, ic * 1024:(ic + 1) * 1024]))
                    t_cw[(l, half, ic)] = P.dma("gpsimd", waits=[t_ce])(
                        lambda g, l=l, half=half, ic=ic: g.dma_start(
                            out=cscr[l, ic * 128:(ic + 1) * 128, half * 1024:(half + 1) * 1024],
                            in_=cst[:]))
                    t_badd[(l, half, 7)] = P.step("vector", waits=[t_cw[(l, half, ic)]])(
                        lambda ve: ve.tensor_copy(tinyr[0:1, 60:61], tinyr[0:1, 60:61])) \
                        if ic < 15 else t_badd[(l, half, 7)]
            prev_cw_tokens = [t_cw[(l, 1, 15)]]

            w_b2 = [t_badd[(l, 1, 7)]]
            if l > 0:
                w_b2.append(t_pb4)
            t_b2 = P.dma("sync", waits=w_b2)(
                lambda sy, l=l: sy.dma_start(out=rowa[0:1, :], in_=entpb[l:l + 1, :]))
            t_eb = P.step("vector", waits=[t_b2])(
                lambda ve, l=l: ve.tensor_add(entbias_r[0:1, l * 2048:(l + 1) * 2048],
                                              bacc[0:1, 0:2048], rowa[0:1, :]))

            t_phl = P.dma("sync", waits=[t_eb], n=2)(lambda sy, l=l: [
                sy.dma_start(out=rowb[0:1, :], in_=phasev[l, 0:1, :]),
                sy.dma_start(out=rowc[0:1, :], in_=phasev[l, 1:2, :])])
            t_mod = {}
            for p_ in range(2):
                rw = rowb if p_ == 0 else rowc
                t_m1 = P.step("vector", waits=[t_phl])(lambda ve, rw=rw: [
                    ve.tensor_scalar(sct[0:1, 0:512] if False else rw[0:1, :], rw[0:1, :],
                                     1.0 / (2 * PI), 0.125, ALU.mult, ALU.add),
                    ve.tensor_scalar(tinyr[0:1, 50:51] if False else sct[0:1, 0:1], sct[0:1, 0:1],
                                     1.0, None, ALU.mult)][-2])
                t_m2 = P.step("vector", waits=[t_m1])(lambda ve, rw=rw: [
                    ve.tensor_scalar(rowa[0:1, :], rw[0:1, :], RNE, None, ALU.add),
                    ve.tensor_copy(lnst[:, 1:2], lnst[:, 1:2]),
                    ve.tensor_scalar(rowa[0:1, :], rowa[0:1, :], -RNE, None, ALU.add),
                    ve.tensor_copy(lnst[:, 1:2], lnst[:, 1:2]),
                    ve.tensor_sub(rw[0:1, :], rw[0:1, :], rowa[0:1, :])][-1])
                t_m3 = P.step("scalar", waits=[t_m2])(
                    lambda ac, rw=rw: ac.activation(rw[0:1, :], rw[0:1, :], AF.Sin,
                                                    scale=2 * PI))
                t_m4 = P.step("vector", waits=[t_m3])(
                    lambda ve, rw=rw: ve.tensor_scalar(rw[0:1, :], rw[0:1, :], SQRT2, None, ALU.mult))
                t_mod[p_] = t_m4
                t_mb = P.step("tensor", waits=[t_m4, t_badd[(l, 1, 7)]])(
                    lambda te, rw=rw: [te.matmul(fp[c][:, 0:512], ones_f[0:1, :],
                                                 rw[0:1, c * 512:(c + 1) * 512],
                                                 start=True, stop=True) for c in range(4)][-1])
                t_mbe = P.step("vector", waits=[t_mb])(
                    lambda ve, p_=p_: [ve.tensor_copy(
                        modbc[:, p_ * 2048 + c * 512:p_ * 2048 + (c + 1) * 512],
                        fp[c][:, 0:512]) for c in range(4)][-1])
                t_mod[p_] = t_mbe
            t_wprev = None
            for dc in range(16):
                w_p = [t_mod[0], t_mod[1]]
                if t_wprev is not None:
                    w_p.append(t_wprev)
                t_pw = P.dma("gpsimd", waits=w_p, n=2)(lambda g, l=l, dc=dc: [
                    g.dma_start(out=pwb[:, 0:2048], in_=pathw[l, 0, dc * 128:(dc + 1) * 128, :]),
                    g.dma_start(out=pwb[:, 2048:4096], in_=pathw[l, 1, dc * 128:(dc + 1) * 128, :])])
                t_wf = P.step("vector", waits=[t_pw])(lambda ve: [
                    ve.tensor_mul(sct[:], pwb[:, 0:512], modbc[:, 0:512]),
                    ve.tensor_mul(sct2[:], pwb[:, 2048:2560], modbc[:, 2048:2560])][-1])
                t_wf2 = []
                for c in range(4):
                    t_wc = P.step("vector", waits=[t_pw] if c > 0 else [t_pw])(
                        lambda ve, c=c: [
                            ve.tensor_mul(sct[:], pwb[:, c * 512:(c + 1) * 512],
                                          modbc[:, c * 512:(c + 1) * 512]),
                            ve.tensor_mul(sct2[:], pwb[:, 2048 + c * 512:2048 + (c + 1) * 512],
                                          modbc[:, 2048 + c * 512:2048 + (c + 1) * 512]),
                            ve.tensor_add(wef[:, c * 512:(c + 1) * 512], sct[:], sct2[:])][-1])
                    t_wf2.append(t_wc)
                t_ww = P.dma("gpsimd", waits=[t_wf2[-1]])(
                    lambda g, l=l, dc=dc: g.dma_start(
                        out=weffscr[l, dc * 128:(dc + 1) * 128, :], in_=wef[:]))
                t_wprev = t_ww
            t_pb = P.dma("sync", waits=[t_mod[1]])(
                lambda sy, l=l: sy.dma_start(out=rowa[0:1, :], in_=pathb[l, 0:1, :]))
            t_pb2 = P.step("vector", waits=[t_pb])(
                lambda ve: ve.tensor_mul(bacc[0:1, 0:2048], rowa[0:1, :], rowb[0:1, :]))
            t_pb3 = P.dma("sync", waits=[t_pb2])(
                lambda sy, l=l: sy.dma_start(out=rowa[0:1, :], in_=pathb[l, 1:2, :]))
            t_pb4 = P.step("vector", waits=[t_pb3])(lambda ve, l=l: [
                ve.tensor_mul(rowa[0:1, :], rowa[0:1, :], rowc[0:1, :]),
                ve.tensor_copy(lnst[:, 1:2], lnst[:, 1:2]),
                ve.tensor_add(pathbias_r[0:1, l * 2048:(l + 1) * 2048],
                              bacc[0:1, 0:2048], rowa[0:1, :])][-1])
            P.flush(block)

        p4_done = P.barrier()
        p4.close()

        p5 = ExitStack()
        wstr = sb(p5, "wstr", [128, 16 * 2048], BF16)
        stg = sb(p5, "stg", [128, 4 * 8192], BF16)
        r1f = sb(p5, "r1f", [128, 4 * 2048], F32)
        thb = r1f
        xn2t = sb(p5, "xn2t", [128, 2048], BF16)
        sqb = stg[:, 8192:16384]
        snb = sb(p5, "snb", [128, 2 * 512], F32)
        g1 = sb(p5, "g1", [128, 512], F32)
        g2 = sb(p5, "g2", [128, 512], F32)
        pp = [ps(p5, f"pp{i}", [128, 512], F32) for i in range(7)]
        trb5 = ps(p5, "trb5", [128, 1024], BF16)
        trb5v = trb5.rearrange("p (k o) -> p k o", o=128)

        stgv = stg.rearrange("p (r dc t) -> p r dc t", dc=16, t=512)
        XN2, AMP, HH, H1 = 0, 1, 2, 3
        r1v = r1f.rearrange("p (s c) -> p s c", c=2048)
        sv = r1f.rearrange("p (jc t) -> p jc t", t=512)

        jgrps5 = [list(range(0, 6)), list(range(6, 12)), list(range(12, 16))]
        prev_ts_ready = None

        def mm_stage(ts, src_r, dst_write, strip_tok, war_tok, dst_kind):
            evtoks = [None] * 16
            for gi, jg in enumerate(jgrps5):
                for jc in jg:
                    w_m = [strip_tok]
                    if war_tok:
                        w_m.append(war_tok)
                    if jc >= 6:
                        if evtoks[jc - 6] is not None:
                            w_m.append(evtoks[jc - 6])
                    t_m = P.step("tensor", waits=w_m)(
                        lambda te, jc=jc, src_r=src_r: [te.matmul(
                            pp[jc % 6][:], wstr[:, dc * 2048 + jc * 128:dc * 2048 + (jc + 1) * 128],
                            stgv[:, src_r, dc, :], start=(dc == 0), stop=(dc == 15))
                            for dc in range(16)][-1] if dst_kind != "bias" else None)
                    evtoks[jc] = dst_write(jc, t_m)
            return evtoks

        for ts in range(4):
            w_r1 = list(p4_done) if ts == 0 else [t_yw]
            t_r1 = P.dma("sync", waits=w_r1, n=4)(
                lambda sy, ts=ts: [sy.dma_start(
                    out=r1v[:, tsub, :],
                    in_=r1scr[ts * 512 + tsub * 128:ts * 512 + (tsub + 1) * 128, :])
                    for tsub in range(4)])
            t_ev5 = None
            t_tr5 = None
            for tsub in range(4):
                xt = r1v[:, tsub, :]
                t_v1 = P.step("vector", waits=[t_r1])(
                    lambda ve, xt=xt: ve.reduce_sum(lnst[:, 0:1], xt, axis=AX.X))
                w_q = [t_r1]
                if t_tr5:
                    w_q.append(t_tr5)
                t_a1 = P.step("scalar", waits=w_q)(
                    lambda ac, xt=xt: ac.activation(xn2t[:], xt, AF.Square, accum_out=lnst[:, 1:2]))
                t_v2 = P.step("vector", waits=[t_a1])(lambda ve: [
                    ve.tensor_scalar(lnst[:, 3:4], lnst[:, 1:2], 1.0 / 2048.0, None, ALU.mult),
                    ve.tensor_scalar(lnst[:, 2:3], lnst[:, 0:1], 1.0 / 2048.0, None, ALU.mult),
                    ve.tensor_copy(lnst[:, 0:1], lnst[:, 0:1]),
                    ve.tensor_mul(lnst[:, 4:5], lnst[:, 2:3], lnst[:, 2:3]),
                    ve.tensor_copy(lnst[:, 0:1], lnst[:, 0:1]),
                    ve.tensor_sub(lnst[:, 5:6], lnst[:, 3:4], lnst[:, 4:5])][-1])
                t_a2 = P.step("scalar", waits=[t_v2])(lambda ac: [
                    ac.activation(lnst[:, 6:7], lnst[:, 5:6], AF.Ln, bias=eps_c[:, 0:1]),
                    ac.activation(lnst[:, 6:7], lnst[:, 6:7], AF.Exp, scale=-0.5)][-1])
                t_v3 = P.step("vector", waits=[t_a2])(lambda ve, xt=xt: [
                    ve.tensor_scalar(lnst[:, 7:8], lnst[:, 2:3], lnst[:, 6:7], -1.0,
                                     ALU.mult, ALU.mult),
                    ve.tensor_copy(lnst[:, 0:1], lnst[:, 0:1]),
                    ve.tensor_scalar(xn2t[:], xt, lnst[:, 6:7], lnst[:, 7:8],
                                     ALU.mult, ALU.add)][-1])
                w_t = [t_v3]
                if ts == 0 and tsub == 0:
                    w_t += p4_done
                if t_ev5:
                    w_t.append(t_ev5)
                t_tA = P.step("tensor", waits=w_t)(
                    lambda te: [te.transpose(trb5v[:, j, :], xn2t[:, j * 128:(j + 1) * 128],
                                             id_bf[:]) for j in range(8)][-1])
                t_eA = P.step("vector", waits=[t_tA])(
                    lambda ve, tsub=tsub: ve.tensor_copy(
                        stgv[:, XN2, 0:8, tsub * 128:(tsub + 1) * 128], trb5v[:]))
                t_tB = P.step("tensor", waits=[t_eA])(
                    lambda te: [te.transpose(trb5v[:, j - 8, :], xn2t[:, j * 128:(j + 1) * 128],
                                             id_bf[:]) for j in range(8, 16)][-1])
                t_tr5 = t_tB
                t_ev5 = P.step("vector", waits=[t_tB])(
                    lambda ve, tsub=tsub: ve.tensor_copy(
                        stgv[:, XN2, 8:16, tsub * 128:(tsub + 1) * 128], trb5v[:]))
            t_xn2 = t_ev5

            w_st = [t_xn2] if ts == 0 else [t_xn2, prev_ts_ready]
            t_stamp = P.dma("gpsimd", waits=w_st, n=4)(
                lambda g: [g.dma_start(
                    out=wstr[:, q * 8192:(q + 1) * 8192]
                    .rearrange("p (dc t) -> p dc t", t=2048),
                    in_=ampw[q * 512:(q + 1) * 512, :].rearrange("(dc p) t -> p dc t", p=128))
                    for q in range(4)])

            def amp_write(jc, t_m):
                return P.step("scalar", waits=[t_m])(
                    lambda ac, jc=jc: ac.activation(
                        stgv[:, AMP, jc, :], pp[jc % 6][:], AF.Sigmoid,
                        bias=ampb_c[:, jc:jc + 1]))
            amp_toks = mm_stage(ts, XN2, amp_write, t_stamp, None, "amp")

            t_stph = P.dma("gpsimd", waits=[amp_toks[15], amp_toks[11]], n=4)(
                lambda g: [g.dma_start(
                    out=wstr[:, q * 8192:(q + 1) * 8192]
                    .rearrange("p (dc t) -> p dc t", t=2048),
                    in_=phw[q * 512:(q + 1) * 512, :].rearrange("(dc p) t -> p dc t", p=128))
                    for q in range(4)])

            tanh_toks = [None] * 16

            def ph_write(jc, t_m):
                w = [t_m, t_r1 if False else t_m]
                tok = P.step("scalar", waits=[t_m])(
                    lambda ac, jc=jc: ac.activation(sv[:, jc, :], pp[jc % 6][:], AF.Tanh,
                                                    bias=phb_c[:, jc:jc + 1]))
                tanh_toks[jc] = tok
                return tok
            ph_toks = mm_stage(ts, XN2, ph_write, t_stph, t_xn2, "ph")

            t_h_toks = []
            t_sin_prev = None
            for jc in range(16):
                t_rn = P.step("vector", waits=[tanh_toks[jc]])(lambda ve, jc=jc: [
                    ve.tensor_scalar(g1[:], sv[:, jc, :], 0.5, 0.125, ALU.mult, ALU.add),
                    ve.tensor_scalar(g2[:], g1[:], RNE, None, ALU.add),
                    ve.tensor_scalar(g2[:], g2[:], -RNE, None, ALU.add),
                    ve.tensor_sub(sv[:, jc, :], g1[:], g2[:])][-1])
                w_sin = [t_rn]
                if t_sin_prev is not None and jc >= 2:
                    w_sin.append(t_h_toks[jc - 2])
                t_sn = P.step("scalar", waits=w_sin)(
                    lambda ac, jc=jc: ac.activation(snb[:, (jc % 2) * 512:(jc % 2 + 1) * 512],
                                                    sv[:, jc, :], AF.Sin, scale=2 * PI))
                t_sin_prev = t_sn
                t_h = P.step("vector", waits=[t_sn])(
                    lambda ve, jc=jc: ve.scalar_tensor_tensor(
                        stgv[:, HH, jc, :], snb[:, (jc % 2) * 512:(jc % 2 + 1) * 512], SQRT2,
                        stgv[:, AMP, jc, :], ALU.mult, ALU.mult))
                t_h_toks.append(t_h)
            t_hlast = t_h_toks[-1]

            h_src = HH
            h_dst = H1
            prev_ready = t_hlast
            for l in range(2):
                t_stc = P.dma("sync", waits=[prev_ready], n=4)(
                    lambda sy, l=l: [sy.dma_start(
                        out=wstr[:, q * 8192:(q + 1) * 8192]
                        .rearrange("p (dc t) -> p dc t", t=2048),
                        in_=cscr[l, q * 512:(q + 1) * 512, :].rearrange("(dc p) t -> p dc t", p=128))
                        for q in range(4)])

                s_toks = [None] * 16

                def ent_write(jc, t_m, l=l, h_src=h_src):
                    t_b = P.step("tensor", waits=[t_m] if False else [])(
                        lambda te, jc=jc, l=l: te.matmul(
                            pp[jc % 6][:], entbias_r[0:1, l * 2048 + jc * 128:l * 2048 + (jc + 1) * 128],
                            ones_b[0:1, 0:512], start=False, stop=True))
                    tok = P.step("vector", waits=[t_b])(
                        lambda ve, jc=jc, h_src=h_src: ve.tensor_add(
                            sv[:, jc, :], pp[jc % 6][:], stgv[:, h_src, jc, :]))
                    s_toks[jc] = tok
                    return tok
                ent_toks = mm_stage(ts, h_src, ent_write, t_stc, prev_ready, "ent")
                h_out1 = ln_apply = None
                t_sq_toks = []
                for jc in range(16):
                    t_sqj = P.step("scalar", waits=[ent_toks[jc]])(
                        lambda ac, jc=jc: ac.activation(
                            sqb[:, jc * 512:(jc + 1) * 512], sv[:, jc, :], AF.Square))
                    t_sq_toks.append(t_sqj)
                t_sum = P.step("tensor", waits=[ent_toks[15], ent_toks[11], ent_toks[5]])(
                    lambda te: [te.matmul(pp[0][0:1, :], ones_f[:, 0:1], sv[:, jc, :],
                                          start=(jc == 0), stop=(jc == 15)) for jc in range(16)][-1])
                t_sum2 = P.step("tensor", waits=[t_sq_toks[-1]])(
                    lambda te: [te.matmul(pp[1][0:1, :], ones_b[:, 0:1],
                                          sqb[:, jc * 512:(jc + 1) * 512],
                                          start=(jc == 0), stop=(jc == 15)) for jc in range(16)][-1])
                t_rw = P.step("vector", waits=[t_sum, t_sum2])(lambda ve: [
                    ve.tensor_scalar(g1[0:1, :], pp[0][0:1, :], 1.0 / 2048.0, None, ALU.mult),
                    ve.tensor_scalar(g2[0:1, :], pp[1][0:1, :], 1.0 / 2048.0, None, ALU.mult),
                    ve.tensor_mul(sct[0:1, :], g1[0:1, :], g1[0:1, :]),
                    ve.tensor_copy(lnst[:, 1:2], lnst[:, 1:2]),
                    ve.tensor_sub(g2[0:1, :], g2[0:1, :], sct[0:1, :])][-1])
                t_rr = P.step("scalar", waits=[t_rw])(lambda ac: [
                    ac.activation(g2[0:1, :], g2[0:1, :], AF.Ln, bias=eps_c[0:1, 0:1]),
                    ac.activation(g2[0:1, :], g2[0:1, :], AF.Exp, scale=-0.5)][-1])
                t_mr = P.step("vector", waits=[t_rr])(
                    lambda ve: ve.tensor_mul(g1[0:1, :], g1[0:1, :], g2[0:1, :]))
                t_bc1 = P.step("tensor", waits=[t_mr])(lambda te: [
                    te.matmul(pp[2][:], ones_f[0:1, :], g2[0:1, :], start=True, stop=True),
                    te.matmul(pp[3][:], ones_f[0:1, :], g1[0:1, :], start=True, stop=True)][-1])
                ap_toks = []
                for jc in range(16):
                    gcol = lng_c[:, (l * 2) * 16 + jc:(l * 2) * 16 + jc + 1]
                    bcol = lnb_c[:, (l * 2) * 16 + jc:(l * 2) * 16 + jc + 1]
                    t_ap = P.step("vector", waits=[t_bc1])(lambda ve, jc=jc, gcol=gcol, bcol=bcol,
                                                           h_dst=h_dst: [
                        ve.tensor_mul(sct[:], sv[:, jc, :], pp[2][:]),
                        ve.tensor_sub(sct[:], sct[:], pp[3][:]),
                        ve.tensor_scalar(stgv[:, h_dst, jc, :], sct[:], gcol, bcol,
                                         ALU.mult, ALU.add)][-1])
                    ap_toks.append(t_ap)
                t_stw = P.dma("sync", waits=[ap_toks[15]], n=4)(
                    lambda sy, l=l: [sy.dma_start(
                        out=wstr[:, q * 8192:(q + 1) * 8192]
                        .rearrange("p (dc t) -> p dc t", t=2048),
                        in_=weffscr[l, q * 512:(q + 1) * 512, :]
                        .rearrange("(dc p) t -> p dc t", p=128))
                        for q in range(4)])

                sp_toks = [None] * 16

                def path_write(jc, t_m, l=l, h_dst=h_dst):
                    t_b = P.step("tensor", waits=[])(
                        lambda te, jc=jc, l=l: te.matmul(
                            pp[jc % 6][:], pathbias_r[0:1, l * 2048 + jc * 128:l * 2048 + (jc + 1) * 128],
                            ones_b[0:1, 0:512], start=False, stop=True))
                    tok = P.step("vector", waits=[t_b])(
                        lambda ve, jc=jc, h_dst=h_dst: ve.tensor_add(
                            sv[:, jc, :], pp[jc % 6][:], stgv[:, h_dst, jc, :]))
                    sp_toks[jc] = tok
                    return tok
                path_toks = mm_stage(ts, h_dst, path_write, t_stw, ap_toks[-1], "path")
                t_sq_toks2 = []
                for jc in range(16):
                    t_sqj = P.step("scalar", waits=[path_toks[jc]])(
                        lambda ac, jc=jc: ac.activation(
                            sqb[:, jc * 512:(jc + 1) * 512], sv[:, jc, :], AF.Square))
                    t_sq_toks2.append(t_sqj)
                t_sum = P.step("tensor", waits=[path_toks[15], path_toks[11], path_toks[5]])(
                    lambda te: [te.matmul(pp[0][0:1, :], ones_f[:, 0:1], sv[:, jc, :],
                                          start=(jc == 0), stop=(jc == 15)) for jc in range(16)][-1])
                t_sum2 = P.step("tensor", waits=[t_sq_toks2[-1]])(
                    lambda te: [te.matmul(pp[1][0:1, :], ones_b[:, 0:1],
                                          sqb[:, jc * 512:(jc + 1) * 512],
                                          start=(jc == 0), stop=(jc == 15)) for jc in range(16)][-1])
                t_rw = P.step("vector", waits=[t_sum, t_sum2])(lambda ve: [
                    ve.tensor_scalar(g1[0:1, :], pp[0][0:1, :], 1.0 / 2048.0, None, ALU.mult),
                    ve.tensor_scalar(g2[0:1, :], pp[1][0:1, :], 1.0 / 2048.0, None, ALU.mult),
                    ve.tensor_mul(sct[0:1, :], g1[0:1, :], g1[0:1, :]),
                    ve.tensor_copy(lnst[:, 1:2], lnst[:, 1:2]),
                    ve.tensor_sub(g2[0:1, :], g2[0:1, :], sct[0:1, :])][-1])
                t_rr = P.step("scalar", waits=[t_rw])(lambda ac: [
                    ac.activation(g2[0:1, :], g2[0:1, :], AF.Ln, bias=eps_c[0:1, 0:1]),
                    ac.activation(g2[0:1, :], g2[0:1, :], AF.Exp, scale=-0.5)][-1])
                t_mr = P.step("vector", waits=[t_rr])(
                    lambda ve: ve.tensor_mul(g1[0:1, :], g1[0:1, :], g2[0:1, :]))
                t_bc1 = P.step("tensor", waits=[t_mr])(lambda te: [
                    te.matmul(pp[2][:], ones_f[0:1, :], g2[0:1, :], start=True, stop=True),
                    te.matmul(pp[3][:], ones_f[0:1, :], g1[0:1, :], start=True, stop=True)][-1])
                ap2 = []
                for jc in range(16):
                    gcol = lng_c[:, (l * 2 + 1) * 16 + jc:(l * 2 + 1) * 16 + jc + 1]
                    bcol = lnb_c[:, (l * 2 + 1) * 16 + jc:(l * 2 + 1) * 16 + jc + 1]
                    t_ap = P.step("vector", waits=[t_bc1])(lambda ve, jc=jc, gcol=gcol, bcol=bcol,
                                                           h_src=h_src: [
                        ve.tensor_mul(sct[:], sv[:, jc, :], pp[2][:]),
                        ve.tensor_sub(sct[:], sct[:], pp[3][:]),
                        ve.tensor_scalar(stgv[:, h_src, jc, :], sct[:], gcol, bcol,
                                         ALU.mult, ALU.add)][-1])
                    ap2.append(t_ap)
                prev_ready = ap2[-1]
            t_r1b = P.dma("sync", waits=[prev_ready, ap2[0]], n=4)(
                lambda sy, ts=ts: [sy.dma_start(
                    out=r1v[:, tsub, :],
                    in_=r1scr[ts * 512 + tsub * 128:ts * 512 + (tsub + 1) * 128, :])
                    for tsub in range(4)])
            t_fa_prev = None
            fa_toks = []
            for jc in range(16):
                w_ftr = [ap2[jc] if False else ap2[-1], t_r1b]
                if t_fa_prev is not None:
                    w_ftr.append(t_fa_prev)
                t_ftr = P.step("tensor", waits=w_ftr)(
                    lambda te, jc=jc: [te.transpose(trb5v[:, tsub, :],
                                                    stgv[:, HH, jc, tsub * 128:(tsub + 1) * 128],
                                                    id_bf[:]) for tsub in range(4)][-1])
                t_fa = P.step("vector", waits=[t_ftr, t_r1b])(
                    lambda ve, jc=jc: ve.tensor_add(
                        r1v[:, :, jc * 128:(jc + 1) * 128],
                        r1v[:, :, jc * 128:(jc + 1) * 128],
                        trb5v[:, 0:4, :]))
                t_fa_prev = t_fa
                fa_toks.append(t_fa)
            prev_ts_ready = fa_toks[-1]
            t_yw = P.dma("gpsimd", waits=fa_toks)(
                lambda g, ts=ts: g.dma_start(
                    out=y_out[ts * 512:(ts + 1) * 512, :].rearrange("(s p) c -> p s c", p=128),
                    in_=r1v[:]))
            P.flush(block)

        yw_tok = (P.qsem["gpsimd"], P.qsem["gpsimd"].n)

        @block.sync
        def _(sy):
            sy.wait_ge(yw_tok[0].h, yw_tok[1])

        p5.close()
    return nc


def _host_inputs(inputs, core):
    b, hf = core // 2, core % 2
    f32 = np.float32
    x = np.asarray(inputs["L0_attn_x"], dtype=f32)
    d = {}
    d["x0"] = np.ascontiguousarray(x[b, hf * T_:(hf + 1) * T_, :])
    d["x1"] = np.ascontiguousarray(x[b, (1 - hf) * T_:(2 - hf) * T_, :])
    d["wqkv"] = np.ascontiguousarray(np.asarray(inputs["L0_attn_w_qkv"], f32))
    d["wo_w"] = np.ascontiguousarray(np.asarray(inputs["L0_attn_wo"], f32))
    lam = np.stack([np.asarray(inputs["L0_attn_lambda_q1"], f32),
                    np.asarray(inputs["L0_attn_lambda_k1"], f32),
                    np.asarray(inputs["L0_attn_lambda_q2"], f32),
                    np.asarray(inputs["L0_attn_lambda_k2"], f32)], axis=1)
    d["lamv"] = np.ascontiguousarray(lam)
    gam = np.asarray(inputs["L0_attn_gamma"], f32)
    bet = np.asarray(inputs["L0_attn_beta"], f32)
    d["grep"] = np.repeat(gam[hf * 1024:(hf + 1) * 1024], 2)[None, :].copy()
    d["brep"] = np.repeat(bet[hf * 1024:(hf + 1) * 1024], 2)[None, :].copy()
    d["ampw"] = np.ascontiguousarray(np.asarray(inputs["L0_ffn_amp_w"], f32))
    d["ampb"] = np.asarray(inputs["L0_ffn_amp_b"], f32)[None, :].copy()
    d["phw"] = np.ascontiguousarray(np.asarray(inputs["L0_ffn_ph_w"], f32))
    d["phb"] = np.asarray(inputs["L0_ffn_ph_b"], f32)[None, :].copy()
    d["entw"] = np.ascontiguousarray(np.asarray(inputs["L0_ffn_ent_w"], f32))
    d["entb"] = np.ascontiguousarray(np.asarray(inputs["L0_ffn_ent_b"], f32))
    d["entpw"] = np.ascontiguousarray(np.asarray(inputs["L0_ffn_ent_pw"], f32))
    d["entpb"] = np.ascontiguousarray(np.asarray(inputs["L0_ffn_ent_pb"], f32))
    d["pathw"] = np.ascontiguousarray(np.asarray(inputs["L0_ffn_path_w"], f32))
    d["pathb"] = np.ascontiguousarray(np.asarray(inputs["L0_ffn_path_b"], f32))
    d["phasev"] = np.ascontiguousarray(np.asarray(inputs["L0_ffn_phase"], f32))
    d["l3g"] = np.ascontiguousarray(np.asarray(inputs["L0_ffn_ln3_g"], f32))
    d["l3b"] = np.ascontiguousarray(np.asarray(inputs["L0_ffn_ln3_b"], f32))
    d["l4g"] = np.ascontiguousarray(np.asarray(inputs["L0_ffn_ln4_g"], f32))
    d["l4b"] = np.ascontiguousarray(np.asarray(inputs["L0_ffn_ln4_b"], f32))

    pos = np.concatenate([np.arange(hf * T_, (hf + 1) * T_),
                          np.arange((1 - hf) * T_, (2 - hf) * T_)]).astype(np.float64)
    inv = 1.0 / 10000.0 ** (np.arange(0, 128, 2, dtype=np.float64) / 128.0)
    ang = pos[None, :] * inv[:, None]
    cos = np.cos(ang); sin = np.sin(ang)
    cos2 = np.concatenate([cos, cos], axis=0).astype(f32)
    sin2 = np.concatenate([sin, sin], axis=0).astype(f32)
    d["ropetab"] = np.ascontiguousarray(
        np.stack([cos2 * np.float32(SCALER), sin2 * np.float32(SCALER), cos2, sin2], axis=0))

    mb = np.zeros((4, 128, 512), dtype=f32)
    for r in range(4):
        kpos = r * 128 + np.arange(128)
        mb[r] = (kpos[:, None] > np.arange(512)[None, :]).astype(f32)
    d["mbands"] = mb.astype(BF)
    d["mo"] = np.array([[1.0 if hf == 0 else 0.0]], dtype=f32)
    d["idbf"] = np.eye(128, dtype=BF)
    d["onesf_in"] = np.ones((128, 128), dtype=f32)
    d["onesb_in"] = np.ones((128, 512), dtype=BF)
    mt = np.zeros((128, 128), dtype=f32)
    for j in range(64):
        mt[j, j + 64] = 1.0
    for j in range(64, 128):
        mt[j, j - 64] = -1.0
    d["mtrot"] = mt.astype(BF)
    return d


def kernel(**inputs):
    if "nc" not in _CACHE:
        _CACHE["nc"] = build_program()
    nc = _CACHE["nc"]
    in_maps = [_host_inputs(inputs, c) for c in range(8)]
    res = run_bass_kernel_spmd(nc, in_maps, core_ids=list(range(8)))
    out = np.empty((4, 4096, 2048), dtype=np.float32)
    for c in range(8):
        b, hf = c // 2, c % 2
        out[b, hf * T_:(hf + 1) * T_, :] = res.results[c]["y"]
    return out


if __name__ == "__main__":
    import reference as _r
    ins = _r.setup_inputs(0)
    y = kernel(**{k: np.asarray(v) for k, v in ins.items()})
    ref = np.asarray(_r.reference(**ins))
    num = float(np.mean((y - ref) ** 2)); den = float(np.mean(ref ** 2))
    print("rel var ratio:", num / den)


# version 06 (final)
# speedup vs baseline: 1.2819x; 1.2819x over previous
import numpy as np
import ml_dtypes
from contextlib import ExitStack
import concourse.bass as bass
import concourse.mybir as mybir
from concourse.bass_utils import run_bass_kernel_spmd

F32 = mybir.dt.float32
BF16 = mybir.dt.bfloat16
AF = mybir.ActivationFunctionType
ALU = mybir.AluOpType
AX = mybir.AxisListType
BF = ml_dtypes.bfloat16

E_ = 2048
H_ = 16
T_ = 2048
SCALER = float(1.0 / np.sqrt(128.0))
EPS = 1e-5
RNE = 12582912.0
PI = float(np.pi)
SQRT2 = float(np.sqrt(2.0))
INIT_LAMBDA = 0.8

DEBUG = False
TRUNC = None
_CACHE = {}


class Sem:
    def __init__(self, stack, nc, name):
        self.h = stack.enter_context(nc.semaphore(name))
        self.n = 0


class Prog:
    ENGS = ("sync", "scalar", "vector", "tensor", "gpsimd")

    def __init__(self, stack, nc):
        self.nc = nc
        self.esem = {e: Sem(stack, nc, f"es_{e}") for e in self.ENGS}
        self.qsem = {e: Sem(stack, nc, f"qs_{e}") for e in ("sync", "scalar", "gpsimd")}
        self.pend = {e: [] for e in self.ENGS}

    def step(self, eng, waits=()):
        sem = self.esem[eng]

        def deco(fn):
            sem.n += 1
            self.pend[eng].append((list(waits), fn, sem, 1))
            return (sem, sem.n)
        return deco

    def dma(self, eng, waits=(), n=1):
        sem = self.qsem[eng]

        def deco(fn):
            sem.n += 16 * n
            self.pend[eng].append((list(waits), fn, sem, 16))
            return (sem, sem.n)
        return deco

    def raw(self, eng, waits=()):
        def deco(fn):
            self.pend[eng].append((list(waits), fn, None, 0))
            return None
        return deco

    def barrier(self):
        toks = [(s, s.n) for s in self.esem.values() if s.n > 0]
        toks += [(s, s.n) for s in self.qsem.values() if s.n > 0]
        return toks

    def flush(self, block):
        for e in self.ENGS:
            steps = self.pend[e]
            if not steps:
                continue
            dec = getattr(block, e)
            own = self.esem[e]

            @dec
            def _(eng, steps=steps, own=own):
                for waits, fn, sem, inc in steps:
                    for (ws, wv) in waits:
                        if ws is not own:
                            eng.wait_ge(ws.h, wv)
                    out = fn(eng)
                    if sem is None:
                        continue
                    if isinstance(out, list):
                        for ins in out:
                            ins.then_inc(sem.h, inc)
                    else:
                        out.then_inc(sem.h, inc)
            self.pend[e] = []


def kcls(kc, qs):
    if kc >= 16:
        return 2
    if kc < qs * 4:
        return 0
    if kc < qs * 4 + 4:
        return 3
    return 1


def build_program():
    nc = bass.Bass("TRN2", target_bir_lowering=False)

    def din(name, shape, dt=F32):
        return nc.dram_tensor(name, shape, dt, kind="ExternalInput").ap()

    x0 = din("x0", [T_, E_])
    x1 = din("x1", [T_, E_])
    wqkv = din("wqkv", [3 * E_, E_])
    wo_w = din("wo_w", [E_, E_])
    lamv = din("lamv", [128, 4])
    grep = din("grep", [1, E_])
    brep = din("brep", [1, E_])
    ampw = din("ampw", [E_, E_])
    ampb = din("ampb", [1, E_])
    phw = din("phw", [E_, E_])
    phb = din("phb", [1, E_])
    entwT = din("entwT", [2, 4 * E_, 257])
    entpw = din("entpw", [2, 4 * E_, E_])
    entpb = din("entpb", [2, E_])
    pathwS = din("pathwS", [2, 2, 256, E_])
    pathb = din("pathb", [2, 2, E_])
    phasev = din("phasev", [2, 2, E_])
    l3g = din("l3g", [2, E_])
    l3b = din("l3b", [2, E_])
    l4g = din("l4g", [2, E_])
    l4b = din("l4b", [2, E_])
    ropetab = din("ropetab", [4, 128, 4096])
    mbands = din("mbands", [4, 128, 512], BF16)
    mo_in = din("mo", [1, 1])
    idbf = din("idbf", [128, 128], BF16)
    onesf_in = din("onesf_in", [128, 128])
    onesb_in = din("onesb_in", [128, 512], BF16)
    mtrot = din("mtrot", [128, 128], BF16)

    y_out = nc.dram_tensor("y", [T_, E_], F32, kind="ExternalOutput").ap()
    skind = "ExternalOutput" if DEBUG else "Internal"

    def dscr(name, shape, dt):
        return nc.dram_tensor(name, shape, dt, kind=skind).ap()

    wqkv_bf = dscr("wqkv_bf", [3 * E_, E_], BF16)
    wo_bf = dscr("wo_bf", [E_, E_], BF16)
    qscr = dscr("qscr", [H_, 128, T_], BF16)
    kscr = dscr("kscr", [H_, 128, 4096], BF16)
    vscr = dscr("vscr", [H_, 128, 4096], BF16)
    attnscr = dscr("attnscr", [T_, E_], BF16)
    r1scr = dscr("r1scr", [T_, E_], F32)
    cpart = [nc.dram_tensor(f"cpart{l}", [256, E_], BF16).ap() for l in range(2)]
    cgath = [nc.dram_tensor(f"cgath{l}", [E_, E_], BF16).ap() for l in range(2)]
    wefpart = [nc.dram_tensor(f"wefpart{l}", [256, E_], BF16).ap() for l in range(2)]
    wefgath = [nc.dram_tensor(f"wefgath{l}", [E_, E_], BF16).ap() for l in range(2)]
    dbgxn = dscr("dbgxn", [128, 16 * 2048], BF16) if DEBUG else None
    dbgln = dscr("dbgln", [128, 16 * 8], F32) if DEBUG else None

    root = ExitStack()
    with root:
        def sb(stk, name, shape, dt):
            return stk.enter_context(nc.sbuf_tensor(name, shape, dt))

        def ps(stk, name, shape, dt=F32):
            return stk.enter_context(nc.psum_tensor(name, shape, dt))

        block = root.enter_context(nc.Block())
        P = Prog(root, nc)
        ccs = Sem(root, nc, "ccs")

        ones_f = sb(root, "ones_f", [128, 128], F32)
        ones_b = sb(root, "ones_b", [128, 512], BF16)
        id_bf = sb(root, "id_bf", [128, 128], BF16)
        mt_rot = sb(root, "mt_rot", [128, 128], BF16)
        lam_sb = sb(root, "lam_sb", [128, 4], F32)
        lam_col = sb(root, "lam_col", [128, 1], F32)
        mo_col = sb(root, "mo_col", [128, 1], F32)
        grep_c = sb(root, "grep_c", [128, 16], F32)
        brep_c = sb(root, "brep_c", [128, 16], F32)
        ampb_c = sb(root, "ampb_c", [128, 16], F32)
        phb_c = sb(root, "phb_c", [128, 16], F32)
        lng_c = sb(root, "lng_c", [128, 64], F32)
        lnb_c = sb(root, "lnb_c", [128, 64], F32)
        entbias_r = sb(root, "entbias_r", [1, 2 * E_], BF16)
        pathbias_r = sb(root, "pathbias_r", [1, 2 * E_], BF16)
        rowa = sb(root, "rowa", [1, 2048], F32)
        tinyr = sb(root, "tinyr", [1, 64], F32)
        nbias = sb(root, "nbias", [128, 8], F32)
        lnst = sb(root, "lnst", [128, 8], F32)
        dbgln_sb = sb(root, "dbgln_sb", [128, 128], F32)
        stat4 = sb(root, "stat4", [128, 4], F32)
        mcol = sb(root, "mcol", [128, 1], F32)
        eps_c = sb(root, "eps_c", [128, 1], F32)
        t38_c = sb(root, "t38_c", [128, 1], F32)
        rcol = sb(root, "rcol", [128, 1], F32)
        sccol = sb(root, "sccol", [128, 2], F32)
        sct = sb(root, "sct", [128, 512], F32)
        sct2 = sb(root, "sct2", [128, 512], F32)

        p0 = ExitStack()
        pc = ps(p0, "pc", [128, 64], F32)

        cast_jobs = []
        for i in range(6):
            cast_jobs.append((wqkv_bf[i * 1024:(i + 1) * 1024, :], wqkv[i * 1024:(i + 1) * 1024, :]))
        cast_jobs.append((wo_bf[0:1024, :], wo_w[0:1024, :]))
        cast_jobs.append((wo_bf[1024:2048, :], wo_w[1024:2048, :]))

        t_cast = P.dma("gpsimd", n=len(cast_jobs))(
            lambda g, jobs=cast_jobs: [g.dma_start(out=o, in_=i) for (o, i) in jobs])
        tok_cast_wqkv = (P.qsem["gpsimd"], 6 * 16)
        tok_cast_wo = (P.qsem["gpsimd"], 8 * 16)

        t_c0 = P.dma("sync", n=7)(lambda sy: [
            sy.dma_start(out=ones_f[:], in_=onesf_in),
            sy.dma_start(out=ones_b[:], in_=onesb_in),
            sy.dma_start(out=id_bf[:], in_=idbf),
            sy.dma_start(out=mt_rot[:], in_=mtrot),
            sy.dma_start(out=lam_sb[:], in_=lamv),
            sy.dma_start(out=mo_col[0:1, :], in_=mo_in),
            sy.dma_start(out=rowa[0:1, :], in_=grep),
        ])

        t_eps = P.step("vector", waits=[])(lambda ve: [
            ve.memset(eps_c[:], EPS), ve.memset(t38_c[:], 1e-38)][-1])
        t_dots = P.step("tensor", waits=[t_c0])(lambda te: [
            te.matmul(pc[0:1, 0:1], lam_sb[:, 0:1], lam_sb[:, 1:2], start=True, stop=True),
            te.matmul(pc[0:1, 1:2], lam_sb[:, 2:3], lam_sb[:, 3:4], start=True, stop=True)][-1])
        t_exp = P.step("scalar", waits=[t_dots])(
            lambda ac: ac.activation(tinyr[0:1, 0:2], pc[0:1, 0:2], AF.Exp))
        t_lam = P.step("vector", waits=[t_exp])(lambda ve: [
            ve.tensor_sub(tinyr[0:1, 2:3], tinyr[0:1, 0:1], tinyr[0:1, 1:2]),
            ve.tensor_copy(tinyr[0:1, 8:9], tinyr[0:1, 8:9]),
            ve.tensor_scalar(tinyr[0:1, 3:4], tinyr[0:1, 2:3], 1.0, INIT_LAMBDA, ALU.mult, ALU.add)][-1])
        t_lbc = P.step("tensor", waits=[t_lam])(lambda te: [
            te.matmul(pc[:, 2:3], ones_f[0:1, :], tinyr[0:1, 3:4], start=True, stop=True),
            te.matmul(pc[:, 3:4], ones_f[0:1, :], mo_col[0:1, 0:1], start=True, stop=True)][-1])
        t_lev = P.step("vector", waits=[t_lbc])(lambda ve: [
            ve.tensor_copy(lam_col[:], pc[:, 2:3]),
            ve.tensor_copy(mo_col[:], pc[:, 3:4])][-1])

        r2c_jobs = [
            (grep, grep_c, 0), (brep, brep_c, 0), (ampb, ampb_c, 0), (phb, phb_c, 0),
            (l3g[0:1, :], lng_c, 0), (l4g[0:1, :], lng_c, 16),
            (l3g[1:2, :], lng_c, 32), (l4g[1:2, :], lng_c, 48),
            (l3b[0:1, :], lnb_c, 0), (l4b[0:1, :], lnb_c, 16),
            (l3b[1:2, :], lnb_c, 32), (l4b[1:2, :], lnb_c, 48),
        ]
        t_prev_tr = t_dots
        t_prev_ev = t_lev
        t_prev_ld = t_c0
        for src, dst, off in r2c_jobs:
            t_ld = P.dma("sync", waits=[t_prev_tr])(
                lambda sy, src=src: sy.dma_start(out=rowa[0:1, :], in_=src))
            t_tr = P.step("tensor", waits=[t_ld, t_prev_ev])(
                lambda te: [te.transpose(pc[:, j:j + 1], rowa[0:1, j * 128:(j + 1) * 128],
                                         ones_f[0:1, 0:1]) for j in range(16)][-1])
            t_ev = P.step("vector", waits=[t_tr])(
                lambda ve, dst=dst, off=off: ve.tensor_copy(dst[:, off:off + 16], pc[:, 0:16]))
            t_prev_tr, t_prev_ev = t_tr, t_ev

        P.flush(block)
        p0_done = P.barrier()
        p0.close()

        p1 = ExitStack()
        xn1 = sb(p1, "xn1", [128, 16 * 2048], BF16)
        strips = [sb(p1, "strips0", [128, 16 * 768], BF16),
                  sb(p1, "strips1", [128, 16 * 768], BF16)]
        stage = sb(p1, "stage", [128, 6 * 2048], BF16)
        xst = sb(p1, "xst", [128, 2 * 2048], F32)
        ropes = sb(p1, "ropes", [128, 2 * 1024], F32)
        qkraw = sb(p1, "qkraw", [128, 6 * 512], BF16)
        sqs = sb(p1, "sqs", [128, 2048], BF16)
        mm = [ps(p1, f"mm{i}", [128, 512], F32) for i in range(6)]
        trb = ps(p1, "trb", [128, 2048], BF16)
        rotv = trb[:].bitcast(F32)

        xn1v = xn1.rearrange("p (dc t) -> p dc t", t=2048)
        trbv = trb.rearrange("p (k o) -> p k o", o=128)

        stage_last_dma = {}
        strip_last_mm = {}
        rope_slot_last = {}
        rot_last = {}
        t_app_prev = {}
        t_sq_war = None
        t_trb_war = None
        bank_last_ev = {}
        last_epi = None
        bctr = 0
        for p_i in (0, 1):
            xsrc = x0 if p_i == 0 else x1
            t_tr_tile = {}
            t_ev_tile = {}
            for tile in range(16):
                slot = tile % 2
                w_ld = list(p0_done) if (p_i == 0 and tile < 2) else []
                if slot in t_app_prev:
                    w_ld.append(t_app_prev[slot])
                t_x = P.dma("sync", waits=w_ld)(
                    lambda sy, tile=tile, slot=slot, xsrc=xsrc:
                    sy.dma_start(out=xst[:, slot * 2048:(slot + 1) * 2048],
                                 in_=xsrc[tile * 128:(tile + 1) * 128, :]))
                xt = xst[:, slot * 2048:(slot + 1) * 2048]
                t_v1 = P.step("vector", waits=[t_x])(
                    lambda ve, xt=xt: ve.reduce_sum(lnst[:, 0:1], xt, axis=AX.X))
                w_sq = [t_x]
                if tile > 0:
                    w_sq.append(t_tr_tile[tile - 1])
                elif t_sq_war is not None:
                    w_sq.append(t_sq_war)
                t_a1 = P.step("scalar", waits=w_sq)(
                    lambda ac, xt=xt: ac.activation(sqs[:], xt, AF.Square, accum_out=lnst[:, 1:2]))
                t_v2 = P.step("vector", waits=[t_a1])(lambda ve: [
                    ve.tensor_scalar(lnst[:, 3:4], lnst[:, 1:2], 1.0 / 2048.0, None, ALU.mult),
                    ve.tensor_scalar(lnst[:, 2:3], lnst[:, 0:1], 1.0 / 2048.0, None, ALU.mult),
                    ve.tensor_copy(lnst[:, 0:1], lnst[:, 0:1]),
                    ve.tensor_mul(lnst[:, 4:5], lnst[:, 2:3], lnst[:, 2:3]),
                    ve.tensor_copy(lnst[:, 0:1], lnst[:, 0:1]),
                    ve.tensor_sub(lnst[:, 5:6], lnst[:, 3:4], lnst[:, 4:5])][-1])
                t_a2 = P.step("scalar", waits=[t_v2])(lambda ac: [
                    ac.activation(lnst[:, 6:7], lnst[:, 5:6], AF.Ln, bias=eps_c[:, 0:1]),
                    ac.activation(lnst[:, 6:7], lnst[:, 6:7], AF.Exp, scale=-0.5)][-1])
                t_v3 = P.step("vector", waits=[t_a2])(lambda ve, xt=xt: [
                    ve.tensor_scalar(lnst[:, 7:8], lnst[:, 2:3], lnst[:, 6:7], -1.0,
                                     ALU.mult, ALU.mult),
                    ve.tensor_copy(lnst[:, 0:1], lnst[:, 0:1]),
                    ve.tensor_scalar(sqs[:], xt, lnst[:, 6:7], lnst[:, 7:8],
                                     ALU.mult, ALU.add)][-1])
                if DEBUG and p_i == 0:
                    P.step("vector", waits=[t_v3])(
                        lambda ve, tile=tile: ve.tensor_copy(
                            dbgln_sb[:, tile * 8:(tile + 1) * 8], lnst[:]))
                t_app_prev[slot] = t_v3
                w_tr = [t_v3]
                if tile > 0:
                    w_tr.append(t_ev_tile[tile - 1])
                elif t_trb_war is not None:
                    w_tr.append(t_trb_war)
                t_t1 = P.step("tensor", waits=w_tr)(
                    lambda te: [te.transpose(trbv[:, dc, :], sqs[:, dc * 128:(dc + 1) * 128],
                                             id_bf[:]) for dc in range(16)][-1])
                t_tr_tile[tile] = t_t1
                t_v4 = P.step("vector", waits=[t_t1])(
                    lambda ve, tile=tile: ve.tensor_copy(
                        xn1v[:, :, tile * 128:(tile + 1) * 128], trbv[:]))
                t_ev_tile[tile] = t_v4
            t_sq_war = t_tr_tile[15]
            t_xn_done = [t_ev_tile[ts * 4 + 3] for ts in range(4)]
            if DEBUG and p_i == 0:
                tdmp = P.dma("gpsimd", waits=[t_ev_tile[15]])(
                    lambda g: g.dma_start(out=dbgxn, in_=xn1[:]))
                tdmp2 = P.dma("gpsimd", waits=[tdmp])(
                    lambda g: g.dma_start(out=dbgln, in_=dbgln_sb[:]))
            if TRUNC == "p1a" and p_i == 0:
                P.flush(block)

                @block.sync
                def _(sy, tdmp2=tdmp2):
                    sy.wait_ge(tdmp2[0].h, tdmp2[1])
                p1.close()
                return nc

            if p_i == 0:
                batches = [("q", 0, 6), ("q", 6, 6), ("q", 12, 4),
                           ("k", 16, 6), ("k", 22, 6), ("k", 28, 4),
                           ("v", 32, 6), ("v", 38, 6), ("v", 44, 4)]
            else:
                batches = [("k", 16, 6), ("k", 22, 6), ("k", 28, 4),
                           ("v", 32, 6), ("v", 38, 6), ("v", 44, 4)]
            for kind, jc0, n_jc in batches:
                ssl = bctr % 2
                bctr += 1
                w_s = [tok_cast_wqkv]
                if ssl in strip_last_mm:
                    w_s.append(strip_last_mm[ssl])
                t_strip = P.dma("scalar", waits=w_s, n=16)(
                    lambda sc, jc0=jc0, n_jc=n_jc, ssl=ssl: [
                        sc.dma_start_transpose(
                            strips[ssl][:, dc * 768:dc * 768 + n_jc * 128],
                            wqkv_bf[jc0 * 128:(jc0 + n_jc) * 128, dc * 128:(dc + 1) * 128])
                        for dc in range(16)])
                strip = strips[ssl]
                halves = [list(range(0, min(3, n_jc))), list(range(3, n_jc))]
                t_evT = {}
                t_epi = {}
                for ts in range(4):
                    if kind in ("q", "k"):
                        c0 = 0 if kind == "q" else 2
                        rsl = ts % 2
                        w_r = []
                        if rsl in rope_slot_last:
                            w_r.append(rope_slot_last[rsl])
                        t_rsl = P.dma("sync", waits=w_r)(
                            lambda sy, c0=c0, ts=ts, p_i=p_i, rsl=rsl:
                            sy.dma_start(out=ropes[:, rsl * 1024:(rsl + 1) * 1024]
                                         .rearrange("p (c t) -> p c t", t=512),
                                         in_=ropetab[c0:c0 + 2, :,
                                                     p_i * 2048 + ts * 512:p_i * 2048 + (ts + 1) * 512]
                                         .rearrange("c p t -> p c t")))
                    for hi, half in enumerate(halves):
                        if not half:
                            continue
                        w_m = [t_strip, t_xn_done[ts]]
                        if ts > 0:
                            w_m.append(t_evT[(ts - 1, hi)])
                        elif hi in bank_last_ev:
                            w_m.append(bank_last_ev[hi])
                        t_mmh = P.step("tensor", waits=w_m)(
                            lambda te, half=half, strip=strip, ts=ts: [
                                te.matmul(mm[jcb][:],
                                          strip[:, dc * 768 + jcb * 128:dc * 768 + (jcb + 1) * 128],
                                          xn1v[:, dc, ts * 512:(ts + 1) * 512],
                                          start=(dc == 0), stop=(dc == 15))
                                for dc in range(16) for jcb in half][-1])
                        strip_last_mm[ssl] = t_mmh
                        ev_last = None
                        for jcb in half:
                            if kind == "v":
                                w_e = [t_mmh]
                                if jcb in stage_last_dma:
                                    w_e.append(stage_last_dma[jcb])
                                tok = P.step("vector", waits=w_e)(
                                    lambda ve, jcb=jcb, ts=ts: ve.tensor_copy(
                                        stage[:, jcb * 2048 + ts * 512:jcb * 2048 + (ts + 1) * 512],
                                        mm[jcb][:]))
                                ev_last = tok
                                t_epi[(jcb, ts)] = tok
                                last_epi = tok
                            else:
                                t_eq = P.step("vector", waits=[t_mmh])(
                                    lambda ve, jcb=jcb: ve.tensor_copy(
                                        qkraw[:, jcb * 512:(jcb + 1) * 512], mm[jcb][:]))
                                ev_last = t_eq
                                rsl2 = jcb % 2
                                w_rot = [t_eq]
                                if rsl2 in rot_last:
                                    w_rot.append(rot_last[rsl2])
                                t_rot = P.step("tensor", waits=w_rot)(
                                    lambda te, jcb=jcb, rsl2=rsl2: te.matmul(
                                        rotv[:, rsl2 * 512:(rsl2 + 1) * 512],
                                        mt_rot[:], qkraw[:, jcb * 512:(jcb + 1) * 512],
                                        start=True, stop=True))
                                w_rp = [t_rot, t_rsl]
                                if jcb in stage_last_dma:
                                    w_rp.append(stage_last_dma[jcb])
                                tok = P.step("vector", waits=w_rp)(
                                    lambda ve, jcb=jcb, ts=ts, rsl=rsl, rsl2=rsl2: [
                                        ve.tensor_mul(sct[:], qkraw[:, jcb * 512:(jcb + 1) * 512],
                                                      ropes[:, rsl * 1024:rsl * 1024 + 512]),
                                        ve.tensor_mul(sct2[:],
                                                      rotv[:, rsl2 * 512:(rsl2 + 1) * 512],
                                                      ropes[:, rsl * 1024 + 512:rsl * 1024 + 1024]),
                                        ve.tensor_add(
                                            stage[:, jcb * 2048 + ts * 512:jcb * 2048 + (ts + 1) * 512],
                                            sct[:], sct2[:])][-1])
                                rot_last[rsl2] = tok
                                rope_slot_last[rsl] = tok
                                t_epi[(jcb, ts)] = tok
                                last_epi = tok
                        t_evT[(ts, hi)] = ev_last
                for hi in (0, 1):
                    if (3, hi) in t_evT:
                        bank_last_ev[hi] = t_evT[(3, hi)]
                for jcb in range(n_jc):
                    jc = jc0 + jcb
                    if kind == "q":
                        dst = qscr[jc, :, 0:2048]
                    elif kind == "k":
                        dst = kscr[jc - 16, :, p_i * 2048:(p_i + 1) * 2048]
                    else:
                        dst = vscr[jc - 32, :, p_i * 2048:(p_i + 1) * 2048]
                    stage_last_dma[jcb] = P.dma("gpsimd", waits=[t_epi[(jcb, 3)]])(
                        lambda g, jcb=jcb, dst=dst: g.dma_start(
                            out=dst, in_=stage[:, jcb * 2048:(jcb + 1) * 2048]))
            t_trb_war = last_epi
            P.flush(block)

        p1_done = P.barrier()
        p1.close()

        p2 = ExitStack()
        pt1 = sb(p2, "pt1", [128, 16384], BF16)
        pt2 = sb(p2, "pt2", [128, 16384], BF16)
        qh = sb(p2, "qh", [128, 2 * 2048], BF16)
        kh = sb(p2, "kh", [128, 2 * 4096], BF16)
        vfm = sb(p2, "vfm", [128, 2 * 4096], BF16)
        vtm = sb(p2, "vtm", [128, 32 * 129], BF16)
        qsq = sb(p2, "qsq", [128, 2048], BF16)
        ksq = sb(p2, "ksq", [128, 4096], BF16)
        obuf = sb(p2, "obuf", [128, 2 * 4 * 129], F32)
        rsb = sb(p2, "rsb", [128, 8], F32)
        atst = sb(p2, "atst", [128, 2 * 512], BF16)
        mband = sb(p2, "mband", [128, 4 * 512], BF16)
        stp = ps(p2, "stp", [128, 2048], F32)
        op1 = ps(p2, "op1", [128, 512], F32)
        op2 = ps(p2, "op2", [128, 512], F32)
        trb2 = ps(p2, "trb2", [128, 1024], BF16)
        aux2 = ps(p2, "aux2", [128, 512], F32)
        trb2v = trb2.rearrange("p (k o) -> p k o", o=128)
        vtmv = vtm.rearrange("p (k o) -> p k o", o=129)

        t_mbl = P.dma("sync", waits=p1_done)(
            lambda sy: sy.dma_start(out=mband.rearrange("p (r q) -> p r q", q=512),
                                    in_=mbands.rearrange("r p q -> p r q")))
        t_vones = P.step("vector", waits=p1_done)(
            lambda ve: ve.memset(vtmv[:, :, 128:129], 1.0))

        t_exp_last = {1: None, 2: None}
        t_pv_last = {1: None, 2: None}
        t_oev_last = {1: None, 2: None}
        t_comb_q = {}
        t_load = {}
        t_vte = {}
        hs = None
        for h in range(H_):
            sl = h % 2
            w_l = list(p1_done) if h == 0 else []
            if h >= 2:
                w_l = [t_pv_last[2], t_knB]
            t_load[h] = P.dma("sync", waits=w_l, n=3)(lambda sy, h=h, sl=sl: [
                sy.dma_start(out=qh[:, sl * 2048:(sl + 1) * 2048], in_=qscr[h]),
                sy.dma_start(out=kh[:, sl * 4096:(sl + 1) * 4096], in_=kscr[h]),
                sy.dma_start(out=vfm[:, sl * 4096:(sl + 1) * 4096], in_=vscr[h])])
            t_vt_ev_prev = t_vones if h == 0 else t_vte[(h - 1, 3)]
            w_vt0 = [t_load[h]]
            if h > 0:
                w_vt0.append(t_pv_last[2])
            for g in range(4):
                w_vt = list(w_vt0) if g == 0 else [t_vte[(h, g - 1)]]
                t_vt = P.step("tensor", waits=w_vt)(
                    lambda te, g=g, sl=sl: [te.transpose(
                        trb2v[:, j, :], vfm[:, sl * 4096 + (g * 8 + j) * 128: sl * 4096 + (g * 8 + j + 1) * 128],
                        id_bf[:]) for j in range(8)][-1])
                t_vte[(h, g)] = P.step("vector", waits=[t_vt, t_vones])(
                    lambda ve, g=g: ve.tensor_copy(vtmv[:, g * 8:(g + 1) * 8, 0:128], trb2v[:]))
            w_sq2 = [t_load[h]]
            if t_exp_last[2] is not None:
                w_sq2.append(t_exp_last[2])
            t_qsq = P.step("scalar", waits=w_sq2)(
                lambda ac, sl=sl: ac.activation(qsq[:], qh[:, sl * 2048:(sl + 1) * 2048], AF.Square))
            t_ksq = P.step("scalar", waits=[t_load[h]])(
                lambda ac, sl=sl: ac.activation(ksq[:], kh[:, sl * 4096:(sl + 1) * 4096], AF.Square))
            t_redprev = None
            for i in (1, 2):
                base = (i - 1) * 64
                tp = (base, 0)
                w_qn = [t_qsq]
                if t_exp_last[2] is not None:
                    w_qn.append(t_exp_last[2])
                if t_redprev is not None:
                    w_qn.append(t_redprev)
                t_qn = P.step("tensor", waits=w_qn)(
                    lambda te, base=base, tp=tp: [te.matmul(
                        stp[0:1, c * 512:(c + 1) * 512], ones_b[base:base + 64, 0:1],
                        qsq[base:base + 64, c * 512:(c + 1) * 512], start=True, stop=True,
                        tile_position=tp) for c in range(4)][-1])
                t_qr = P.step("vector", waits=[t_qn])(
                    lambda ve, i=i: ve.reduce_max(tinyr[0:1, 10 + i:11 + i], stp[0:1, 0:2048], axis=AX.X))
                t_k1 = P.step("tensor", waits=[t_ksq, t_qr])(
                    lambda te, base=base, tp=tp, sl=sl: [te.matmul(
                        stp[0:1, c * 512:(c + 1) * 512], ones_b[base:base + 64, 0:1],
                        ksq[base:base + 64, c * 512:(c + 1) * 512], start=True, stop=True,
                        tile_position=tp) for c in range(4)][-1])
                t_k1r = P.step("vector", waits=[t_k1])(
                    lambda ve: ve.reduce_max(tinyr[0:1, 20:21], stp[0:1, 0:2048], axis=AX.X))
                t_k2 = P.step("tensor", waits=[t_k1r])(
                    lambda te, base=base, tp=tp: [te.matmul(
                        stp[0:1, c * 512:(c + 1) * 512], ones_b[base:base + 64, 0:1],
                        ksq[base:base + 64, (4 + c) * 512:(5 + c) * 512], start=True, stop=True,
                        tile_position=tp) for c in range(4)][-1])
                t_k2r = P.step("vector", waits=[t_k2])(lambda ve, i=i: [
                    ve.reduce_max(tinyr[0:1, 21:22], stp[0:1, 0:2048], axis=AX.X),
                    ve.tensor_copy(lnst[:, 1:2], lnst[:, 1:2]),
                    ve.tensor_max(tinyr[0:1, 22:23], tinyr[0:1, 20:21], tinyr[0:1, 21:22]),
                    ve.tensor_copy(lnst[:, 1:2], lnst[:, 1:2]),
                    ve.tensor_mul(tinyr[0:1, 24 + i:25 + i], tinyr[0:1, 10 + i:11 + i],
                                  tinyr[0:1, 22:23])][-1])
                t_redprev = t_k2r
            t_B = P.step("scalar", waits=[t_redprev])(lambda ac: [
                ac.activation(tinyr[0:1, 30:32], tinyr[0:1, 25:27], AF.Ln, bias=t38_c[0:1, 0:1]),
                ac.activation(tinyr[0:1, 30:32], tinyr[0:1, 30:32], AF.Exp, scale=0.5)][-1])
            t_Bn = P.step("vector", waits=[t_B])(
                lambda ve: ve.tensor_scalar(tinyr[0:1, 32:34], tinyr[0:1, 30:32], -1.0, None, ALU.mult))
            t_Bb = P.step("tensor", waits=[t_Bn])(
                lambda te: te.matmul(aux2[:, 0:2], ones_f[0:1, :], tinyr[0:1, 32:34],
                                     start=True, stop=True))
            t_knB = P.step("vector", waits=[t_Bb])(lambda ve: [
                ve.tensor_copy(nbias[:, 0:1], aux2[:, 0:1]),
                ve.tensor_copy(nbias[:, 4:5], aux2[:, 1:2]),
                ve.tensor_scalar(nbias[:, 1:2], nbias[:, 0:1], 1.0, None, ALU.add),
                ve.tensor_scalar(nbias[:, 5:6], nbias[:, 4:5], 1.0, None, ALU.add),
                ve.tensor_scalar(nbias[:, 2:3], nbias[:, 0:1], mo_col[:], None, ALU.add),
                ve.tensor_scalar(nbias[:, 6:7], nbias[:, 4:5], mo_col[:], None, ALU.add)][-1])

            for qs in range(4):
                t_madd = {}
                t_exp_tok = {1: [], 2: []}
                for kcp in range(16):
                    kc = 2 * kcp
                    cls = kcls(kc, qs)
                    for i in (1, 2):
                        pt = pt1 if i == 1 else pt2
                        base = (i - 1) * 64
                        scol = (i - 1) * 1024
                        w_s = [t_load[h], t_knB]
                        if kcp > 0:
                            w_s.append(t_exp_tok[i][-1])
                        elif t_pv_last[i] is not None and qs > 0:
                            w_s.append(t_pv_last[i])
                        t_s = P.step("tensor", waits=w_s)(
                            lambda te, kc=kc, base=base, scol=scol, sl=sl, qs=qs: [
                                te.matmul(stp[:, scol + j * 512:scol + (j + 1) * 512],
                                          kh[base:base + 64,
                                             sl * 4096 + (kc + j) * 128:sl * 4096 + (kc + j + 1) * 128],
                                          qh[base:base + 64, sl * 2048 + qs * 512:sl * 2048 + (qs + 1) * 512],
                                          start=True, stop=True, tile_position=(base, 0))
                                for j in range(2)][-1])
                        w_e = [t_s]
                        if cls == 3:
                            r = kc - qs * 4
                            t_ma = P.step("vector", waits=[t_s, t_mbl])(
                                lambda ve, r=r, scol=scol: [
                                    ve.tensor_add(stp[:, scol:scol + 512], stp[:, scol:scol + 512],
                                                  mband[:, r * 512:(r + 1) * 512]),
                                    ve.tensor_add(stp[:, scol + 512:scol + 1024],
                                                  stp[:, scol + 512:scol + 1024],
                                                  mband[:, (r + 1) * 512:(r + 2) * 512])][-1])
                            w_e = [t_ma]
                        bcol = (i - 1) * 4 + (0 if cls == 3 else cls)
                        w_exp = list(w_e)
                        if kcp == 0 and t_pv_last[i] is not None:
                            w_exp.append(t_pv_last[i])
                        t_e = P.step("scalar", waits=w_exp)(
                            lambda ac, pt=pt, kc=kc, scol=scol, bcol=bcol, qs=qs:
                            ac.activation(pt[:, kc * 512:(kc + 2) * 512],
                                          stp[:, scol:scol + 1024], AF.Exp,
                                          bias=nbias[:, bcol:bcol + 1]))
                        t_exp_tok[i].append(t_e)
                t_exp_last[1] = t_exp_tok[1][-1]
                t_exp_last[2] = t_exp_tok[2][-1]
                t_oev = {}
                for i in (1, 2):
                    pt = pt1 if i == 1 else pt2
                    op = op1 if i == 1 else op2
                    for qt in range(4):
                        w_pv = [t_exp_tok[i][-1], t_vte[(h, 3)]]
                        if (i, qt - 1) in t_oev:
                            w_pv.append(t_oev[(i, qt - 1)])
                        elif t_oev_last[i] is not None:
                            w_pv.append(t_oev_last[i])
                        t_pv = P.step("tensor", waits=w_pv)(
                            lambda te, pt=pt, op=op, qt=qt: [
                                te.matmul(op[:, 0:129],
                                          pt[:, kc * 512 + qt * 128:kc * 512 + (qt + 1) * 128],
                                          vtm[:, kc * 129:(kc + 1) * 129],
                                          start=(kc == 0), stop=(kc == 31))
                                for kc in range(32)][-1])
                        t_pv_last[i] = t_pv
                        t_oev[(i, qt)] = t_oev_last[i] = P.step("vector", waits=[t_pv])(
                            lambda ve, i=i, qt=qt, op=op: ve.tensor_copy(
                                obuf[:, (i - 1) * 516 + qt * 129:(i - 1) * 516 + (qt + 1) * 129],
                                op[:, 0:129]))
                o1 = obuf[:, 0:516].rearrange("p (s c) -> p s c", c=129)
                o2 = obuf[:, 516:1032].rearrange("p (s c) -> p s c", c=129)
                asl = qs % 2
                w_c = [t_oev[(1, 3)], t_oev[(2, 3)], t_lev]
                if (h, qs - 2) in t_comb_q or (h - 1, qs + 2) in t_comb_q:
                    kk = (h, qs - 2) if (h, qs - 2) in t_comb_q else (h - 1, qs + 2)
                    w_c.append(t_comb_q[kk])
                sct3 = sct.rearrange("p (s c) -> p s c", c=128)
                sct23 = sct2.rearrange("p (s c) -> p s c", c=128)
                t_cmb = P.step("vector", waits=w_c)(lambda ve, o1=o1, o2=o2, asl=asl,
                                                    sct3=sct3, sct23=sct23: [
                    ve.reciprocal(rsb[:, 0:4], o1[:, :, 128]),
                    ve.reciprocal(rsb[:, 4:8], o2[:, :, 128]),
                    ve.tensor_mul(sct3, o1[:, :, 0:128],
                                  rsb[:, 0:4].rearrange("p (s o) -> p s o", o=1)
                                  .broadcast_to([128, 4, 128])),
                    ve.tensor_mul(sct23, o2[:, :, 0:128],
                                  rsb[:, 4:8].rearrange("p (s o) -> p s o", o=1)
                                  .broadcast_to([128, 4, 128])),
                    ve.tensor_scalar(sct2[:], sct2[:], lam_col[:], None, ALU.mult),
                    ve.tensor_sub(atst[:, asl * 512:(asl + 1) * 512], sct[:], sct2[:])][-1])
                tok_a = P.dma("gpsimd", waits=[t_cmb])(
                    lambda g, h=h, qs=qs, asl=asl: g.dma_start(
                        out=attnscr[qs * 512:(qs + 1) * 512, h * 128:(h + 1) * 128]
                        .rearrange("(s p) c -> p s c", p=128),
                        in_=atst[:, asl * 512:(asl + 1) * 512]
                        .rearrange("p (s c) -> p s c", c=128)))
                t_comb_q[(h, qs)] = tok_a
            P.flush(block)

        p2_done = P.barrier()
        p2.close()

        p3 = ExitStack()
        wost = sb(p3, "wost", [128, 16 * 2048], BF16)
        atile = sb(p3, "atile", [128, 2 * 2048], BF16)
        gnb = sb(p3, "gnb", [128, 2 * 2048], BF16)
        gnfm = sb(p3, "gnfm", [128, 16 * 512], BF16)
        r1a = sb(p3, "r1a", [128, 4 * 2048], F32)
        wob = sb(p3, "wob", [128, 512], BF16)
        sq3 = sb(p3, "sq3", [128, 2048], BF16)
        mm3 = [ps(p3, f"mm3_{i}", [128, 512], F32) for i in range(6)]
        trb3 = ps(p3, "trb3", [128, 1024], BF16)
        aux3 = ps(p3, "aux3", [128, 512], F32)
        trb3v = trb3.rearrange("p (k o) -> p k o", o=128)
        gnfmv = gnfm.rearrange("p (dc t) -> p dc t", t=512)

        t_wost = P.dma("scalar", waits=[tok_cast_wo] + p2_done, n=16)(
            lambda sc: [sc.dma_start_transpose(
                wost[:, dc * 2048:(dc + 1) * 2048],
                wo_bf[:, dc * 128:(dc + 1) * 128]) for dc in range(16)])

        t_gtr = {}
        t_gev = {}
        t_wadd = {}
        t_r1w = {}
        t_wtr_prev = None
        t_wadd_prev = None
        for grp in range(4):
            w_x = list(p2_done) if grp == 0 else [t_r1w[grp - 1]]
            t_xr = P.dma("sync", waits=w_x, n=4)(
                lambda sy, grp=grp: [sy.dma_start(
                    out=r1a[:, tsub * 2048:(tsub + 1) * 2048],
                    in_=x0[grp * 512 + tsub * 128:grp * 512 + (tsub + 1) * 128, :])
                    for tsub in range(4)])
            for pair in range(2):
                t_at = {}
                t_stats = {}
                for t2 in range(2):
                    tile = grp * 4 + pair * 2 + t2
                    slot = tile % 2
                    w_a = list(p2_done) if tile < 2 else [t_gtr[tile - 2]]
                    t_at[t2] = P.dma("sync", waits=w_a)(
                        lambda sy, tile=tile, slot=slot: sy.dma_start(
                            out=atile[:, slot * 2048:(slot + 1) * 2048],
                            in_=attnscr[tile * 128:(tile + 1) * 128, :]))
                    at = atile[:, slot * 2048:(slot + 1) * 2048]
                    t_s1 = P.step("vector", waits=[t_at[t2]])(
                        lambda ve, at=at, t2=t2: ve.reduce_sum(stat4[:, t2:t2 + 1], at, axis=AX.X))
                    t_s2 = P.step("scalar", waits=[t_at[t2], t_s1])(
                        lambda ac, at=at, t2=t2: ac.activation(
                            sq3[:], at, AF.Square, accum_out=stat4[:, 2 + t2:3 + t2]))
                    t_stats[t2] = t_s2
                t_cmb3 = P.step("tensor", waits=[t_stats[0], t_stats[1]])(
                    lambda te: te.matmul(aux3[0:1, 0:4], ones_f[:, 0:1], stat4[:, 0:4],
                                         start=True, stop=True))
                t_gs = P.step("vector", waits=[t_cmb3])(lambda ve: [
                    ve.tensor_copy(tinyr[0:1, 36:40], aux3[0:1, 0:4]),
                    ve.tensor_copy(lnst[:, 1:2], lnst[:, 1:2]),
                    ve.tensor_add(tinyr[0:1, 40:41], tinyr[0:1, 36:37], tinyr[0:1, 37:38]),
                    ve.tensor_add(tinyr[0:1, 41:42], tinyr[0:1, 38:39], tinyr[0:1, 39:40]),
                    ve.tensor_scalar(tinyr[0:1, 42:43], tinyr[0:1, 40:41], 1.0 / 524288.0, None, ALU.mult),
                    ve.tensor_scalar(tinyr[0:1, 43:44], tinyr[0:1, 41:42], 1.0 / 524288.0, None, ALU.mult),
                    ve.tensor_mul(tinyr[0:1, 44:45], tinyr[0:1, 42:43], tinyr[0:1, 42:43]),
                    ve.tensor_copy(lnst[:, 1:2], lnst[:, 1:2]),
                    ve.tensor_sub(tinyr[0:1, 45:46], tinyr[0:1, 43:44], tinyr[0:1, 44:45])][-1])
                t_gr = P.step("scalar", waits=[t_gs])(lambda ac: [
                    ac.activation(tinyr[0:1, 46:47], tinyr[0:1, 45:46], AF.Ln, bias=eps_c[0:1, 0:1]),
                    ac.activation(tinyr[0:1, 46:47], tinyr[0:1, 46:47], AF.Exp, scale=-0.5)][-1])
                t_gb = P.step("tensor", waits=[t_gr])(
                    lambda te: te.matmul(aux3[:, 8:10], ones_f[0:1, :], tinyr[0:1, 42:43]
                                         if False else tinyr[0:1, 42:44].rearrange("a b -> a b"),
                                         start=True, stop=True))
                t_gb2 = P.step("vector", waits=[t_gb])(lambda ve: [
                    ve.tensor_copy(mcol[:], aux3[:, 8:9])][-1])
                t_gb3 = P.step("tensor", waits=[t_gb2, t_gr])(
                    lambda te: te.matmul(aux3[:, 10:11], ones_f[0:1, :], tinyr[0:1, 46:47],
                                         start=True, stop=True))
                t_gb4 = P.step("vector", waits=[t_gb3])(
                    lambda ve: ve.tensor_copy(rcol[:], aux3[:, 10:11]))
                for t2 in range(2):
                    tile = grp * 4 + pair * 2 + t2
                    slot = tile % 2
                    at = atile[:, slot * 2048:(slot + 1) * 2048]
                    t_sc = P.step("vector", waits=[t_gb4])(lambda ve, tile=tile: [
                        ve.tensor_copy(lnst[:, 1:2], lnst[:, 1:2]),
                        ve.tensor_scalar(sccol[:, 0:1], grep_c[:, tile:tile + 1], rcol[:], 0.2,
                                         ALU.mult, ALU.mult),
                        ve.tensor_scalar(lnst[:, 0:1], brep_c[:, tile:tile + 1], 0.2, None, ALU.mult),
                        ve.tensor_mul(sccol[:, 1:2], sccol[:, 0:1], mcol[:]),
                        ve.tensor_copy(lnst[:, 2:3], lnst[:, 2:3]),
                        ve.tensor_sub(sccol[:, 1:2], lnst[:, 0:1], sccol[:, 1:2]),
                        ve.tensor_copy(lnst[:, 1:2], lnst[:, 1:2])][-1])
                    w_app = [t_sc]
                    if tile >= 2:
                        w_app.append(t_gtr[tile - 2])
                    t_app = P.step("vector", waits=w_app)(
                        lambda ve, at=at, slot=slot: ve.tensor_scalar(
                            gnb[:, slot * 2048:(slot + 1) * 2048], at, sccol[:, 0:1],
                            sccol[:, 1:2], ALU.mult, ALU.add))
                    w_tr = [t_app]
                    if tile > 0:
                        w_tr.append(t_gev[tile - 1])
                    if tile % 4 == 0 and grp > 0:
                        w_tr.append(t_wadd[(grp - 1, 15)])
                    t_tr3a = P.step("tensor", waits=w_tr)(
                        lambda te, slot=slot: [te.transpose(
                            trb3v[:, j, :], gnb[:, slot * 2048 + j * 128:slot * 2048 + (j + 1) * 128],
                            id_bf[:]) for j in range(8)][-1])
                    t_ev3a = P.step("vector", waits=[t_tr3a])(
                        lambda ve, tile=tile: ve.tensor_copy(
                            gnfmv[:, 0:8, (tile % 4) * 128:(tile % 4 + 1) * 128], trb3v[:]))
                    t_tr3b = P.step("tensor", waits=[t_ev3a])(
                        lambda te, slot=slot: [te.transpose(
                            trb3v[:, j - 8, :], gnb[:, slot * 2048 + j * 128:slot * 2048 + (j + 1) * 128],
                            id_bf[:]) for j in range(8, 16)][-1])
                    t_gtr[tile] = t_tr3b
                    t_gev[tile] = P.step("vector", waits=[t_tr3b])(
                        lambda ve, tile=tile: ve.tensor_copy(
                            gnfmv[:, 8:16, (tile % 4) * 128:(tile % 4 + 1) * 128], trb3v[:]))
            jgrps = [list(range(0, 6)), list(range(6, 12)), list(range(12, 16))]
            for jg in jgrps:
                for bi_, jc in enumerate(jg):
                    w_m = [t_wost, t_gev[grp * 4 + 3], t_xr]
                    if jc >= 6:
                        w_m.append(t_wadd[(grp, jc - 6)])
                    elif grp > 0:
                        w_m.append(t_wadd[(grp - 1, 12 + jc if jc < 4 else 6 + jc)])
                    t_wm = P.step("tensor", waits=w_m)(
                        lambda te, jc=jc: [te.matmul(
                            mm3[jc % 6][:], wost[:, dc * 2048 + jc * 128:dc * 2048 + (jc + 1) * 128],
                            gnfmv[:, dc, :], start=(dc == 0), stop=(dc == 15))
                            for dc in range(16)][-1])
                    w_we = [t_wm]
                    if t_wtr_prev is not None:
                        w_we.append(t_wtr_prev)
                    t_wev = P.step("vector", waits=w_we)(
                        lambda ve, jc=jc: ve.tensor_copy(wob[:], mm3[jc % 6][:]))
                    t_wtr = P.step("tensor", waits=[t_wev, t_wadd_prev] if t_wadd_prev
                                   else [t_wev])(
                        lambda te: [te.transpose(trb3v[:, 4 + tsub, :],
                                                 wob[:, tsub * 128:(tsub + 1) * 128], id_bf[:])
                                    for tsub in range(4)][-1])
                    t_wtr_prev = t_wtr
                    t_wa = P.step("vector", waits=[t_wtr])(
                        lambda ve, jc=jc: ve.tensor_add(
                            r1a.rearrange("p (s c) -> p s c", c=2048)[:, :, jc * 128:(jc + 1) * 128],
                            r1a.rearrange("p (s c) -> p s c", c=2048)[:, :, jc * 128:(jc + 1) * 128],
                            trb3v[:, 4:8, :]))
                    t_wadd[(grp, jc)] = t_wa
                    t_wadd_prev = t_wa
            t_r1w[grp] = P.dma("gpsimd", waits=[t_wadd[(grp, j)] for j in range(16)])(
                lambda g, grp=grp: g.dma_start(
                    out=r1scr[grp * 512:(grp + 1) * 512, :].rearrange("(s p) c -> p s c", p=128),
                    in_=r1a.rearrange("p (s c) -> p s c", c=2048)))
            P.flush(block)

        p3_done = P.barrier()
        p3.close()

        p4 = ExitStack()
        astr = [sb(p4, f"astr{i}", [128, 64 * 257], BF16) for i in range(2)]
        bbuf = [sb(p4, f"bbuf{i}", [128, 8 * 512], BF16) for i in range(2)]
        cpsb = [sb(p4, f"cpsb{i}", [128, 2 * 512], BF16) for i in range(2)]
        bacc = sb(p4, "bacc", [1, 2048], F32)
        bacc2 = sb(p4, "bacc2", [1, 2048], F32)
        modscr = sb(p4, "modscr", [1, 2048], F32)
        modbc = sb(p4, "modbc", [128, 2 * 2048], F32)
        pwb = sb(p4, "pwb", [128, 2 * 2048], BF16)
        wef = sb(p4, "wef", [128, 2048], BF16)
        prow = sb(p4, "prow", [1, 2 * 2048], F32)
        rowb = prow[0:1, 0:2048]
        rowc = prow[0:1, 2048:4096]
        psAB = [[ps(p4, f"ps{s}_{i}", [128, 512], F32) for i in range(3)]
                for s in range(2)]
        pbc = ps(p4, "pbc", [128, 512], F32)

        astr_tok = {}
        bb_last = {}
        t_ev_prev = {}
        t_cp_war = {}
        t_eb_prev = None
        t_pb4 = None
        t_rowa_prev = None
        bctr4 = 0
        cc_n = 0
        for l in range(2):
            for part in range(8):
                q = "gpsimd"
                w = list(p3_done) if (l == 0 and part < 2) else []
                astr_tok[(l, part)] = P.dma(q, waits=w)(
                    lambda sy, l=l, part=part: sy.dma_start(
                        out=astr[l].rearrange("p (kc r) -> p kc r", r=257)
                        [:, part * 8:(part + 1) * 8, :],
                        in_=entwT[l, part * 1024:(part + 1) * 1024, :]
                        .rearrange("(kc p) r -> p kc r", p=128)))
            for jb in range(4):
                pset = psAB[jb % 2]
                t_mm = None
                for kcg in range(8):
                    sl = bctr4 % 2
                    bctr4 += 1
                    q = "gpsimd"
                    w_b = []
                    if sl in bb_last:
                        w_b.append(bb_last[sl])
                    elif l == 0 and jb == 0:
                        w_b += p3_done
                    t_bl = P.dma(q, waits=w_b)(
                        lambda sy, l=l, jb=jb, kcg=kcg, sl=sl: sy.dma_start(
                            out=bbuf[sl].rearrange("p (k j) -> p k j", j=512),
                            in_=entpw[l, kcg * 1024:(kcg + 1) * 1024,
                                      jb * 512:(jb + 1) * 512]
                            .rearrange("(k p) j -> p k j", p=128)))
                    w_m = [t_bl, astr_tok[(l, kcg)]]
                    if kcg == 0 and (jb % 2) in t_ev_prev:
                        w_m.append(t_ev_prev[jb % 2])
                    t_mm = P.step("tensor", waits=w_m)(
                        lambda te, l=l, kcg=kcg, sl=sl, pset=pset: [
                            te.matmul(
                                (pset[0][:, :], pset[1][:, :], pset[2][0:1, :])[rc],
                                astr[l][:, (kcg * 8 + k) * 257 + (0, 128, 256)[rc]:
                                        (kcg * 8 + k) * 257 + (128, 256, 257)[rc]],
                                bbuf[sl][:, k * 512:(k + 1) * 512],
                                start=(kcg == 0 and k == 0),
                                stop=(kcg == 7 and k == 7))
                            for k in range(8) for rc in range(3)][-1])
                    bb_last[sl] = t_mm
                csl = jb % 2
                w_e = [t_mm]
                if csl in t_cp_war:
                    w_e.append(t_cp_war[csl])
                if l == 1 and jb == 0 and t_eb_prev is not None:
                    w_e.append(t_eb_prev)
                t_ev = P.step("vector", waits=w_e)(
                    lambda ve, jb=jb, pset=pset, csl=csl: [
                        ve.tensor_copy(cpsb[csl][:, 0:512], pset[0][:, :]),
                        ve.tensor_copy(cpsb[csl][:, 512:1024], pset[1][:, :]),
                        ve.tensor_copy(bacc[0:1, jb * 512:(jb + 1) * 512],
                                       pset[2][0:1, :])][-1])
                t_ev_prev[jb % 2] = t_ev
                t_cp = P.dma("scalar", waits=[t_ev])(
                    lambda sc, l=l, jb=jb, csl=csl: sc.dma_start(
                        out=cpart[l][0:256, jb * 512:(jb + 1) * 512]
                        .rearrange("(rc p) j -> p rc j", p=128),
                        in_=cpsb[csl].rearrange("p (rc j) -> p rc j", j=512)))
                t_cp_war[csl] = t_cp
                t_cp_last = t_cp
            cc_n += 1
            P.raw("gpsimd", waits=[t_cp_last])(
                lambda g, l=l, v=cc_n: g.collective_compute(
                    "AllGather", ALU.bypass, replica_groups=[list(range(8))],
                    ins=[cpart[l].opt()], outs=[cgath[l].opt()])
                .then_inc(ccs.h, 1))

            w_b2 = [t_ev_prev[3 % 2], t_ev_prev[0]]
            if t_rowa_prev is not None:
                w_b2.append(t_rowa_prev)
            t_b2 = P.dma("sync", waits=w_b2)(
                lambda sy, l=l: sy.dma_start(out=rowa[0:1, :], in_=entpb[l:l + 1, :]))
            t_eb = P.step("vector", waits=[t_b2])(
                lambda ve, l=l: ve.tensor_add(entbias_r[0:1, l * 2048:(l + 1) * 2048],
                                              bacc[0:1, 0:2048], rowa[0:1, :]))
            t_eb_prev = t_eb

            w_ph = list(p3_done) if l == 0 else [t_pb4]
            t_phl = P.dma("sync", waits=w_ph, n=2)(lambda sy, l=l: [
                sy.dma_start(out=rowb, in_=phasev[l, 0:1, :]),
                sy.dma_start(out=rowc, in_=phasev[l, 1:2, :])])
            t_mod = {}
            t_me_prev = None
            for p_ in range(2):
                rw = rowb if p_ == 0 else rowc
                t_m1 = P.step("vector", waits=[t_phl])(lambda ve, rw=rw: [
                    ve.tensor_scalar(rw[0:1, :], rw[0:1, :],
                                     1.0 / (2 * PI), 0.125, ALU.mult, ALU.add),
                    ve.tensor_copy(lnst[:, 1:2], lnst[:, 1:2])][-1])
                t_m2 = P.step("vector", waits=[t_m1])(lambda ve, rw=rw: [
                    ve.tensor_scalar(modscr[0:1, :], rw[0:1, :], RNE, None, ALU.add),
                    ve.tensor_copy(lnst[:, 1:2], lnst[:, 1:2]),
                    ve.tensor_scalar(modscr[0:1, :], modscr[0:1, :], -RNE, None, ALU.add),
                    ve.tensor_copy(lnst[:, 1:2], lnst[:, 1:2]),
                    ve.tensor_sub(rw[0:1, :], rw[0:1, :], modscr[0:1, :])][-1])
                t_m3 = P.step("scalar", waits=[t_m2])(
                    lambda ac, rw=rw: ac.activation(rw[0:1, :], rw[0:1, :], AF.Sin,
                                                    scale=2 * PI))
                t_m4 = P.step("vector", waits=[t_m3])(
                    lambda ve, rw=rw: ve.tensor_scalar(rw[0:1, :], rw[0:1, :],
                                                       SQRT2, None, ALU.mult))
                for c in range(4):
                    w_mb = [t_m4] if t_me_prev is None else [t_m4, t_me_prev]
                    t_mb = P.step("tensor", waits=w_mb)(
                        lambda te, rw=rw, c=c: te.matmul(
                            pbc[:, :], ones_f[0:1, :], rw[0:1, c * 512:(c + 1) * 512],
                            start=True, stop=True))
                    t_me_prev = P.step("vector", waits=[t_mb])(
                        lambda ve, p_=p_, c=c: ve.tensor_copy(
                            modbc[:, p_ * 2048 + c * 512:p_ * 2048 + (c + 1) * 512],
                            pbc[:, :]))
                t_mod[p_] = t_me_prev
            t_wprev = None
            for dc in range(2):
                w_p = [t_mod[0], t_mod[1]]
                if t_wprev is not None:
                    w_p.append(t_wprev)
                elif l == 0:
                    w_p += p3_done
                t_pw = P.dma("gpsimd", waits=w_p, n=2)(lambda sy, l=l, dc=dc: [
                    sy.dma_start(out=pwb[:, 0:2048],
                                 in_=pathwS[l, 0, dc * 128:(dc + 1) * 128, :]),
                    sy.dma_start(out=pwb[:, 2048:4096],
                                 in_=pathwS[l, 1, dc * 128:(dc + 1) * 128, :])])
                t_wf2 = []
                for c in range(4):
                    t_wc = P.step("vector", waits=[t_pw] if c == 0 else [t_wf2[-1]])(
                        lambda ve, c=c: [
                            ve.tensor_mul(sct[:], pwb[:, c * 512:(c + 1) * 512],
                                          modbc[:, c * 512:(c + 1) * 512]),
                            ve.tensor_mul(sct2[:], pwb[:, 2048 + c * 512:2048 + (c + 1) * 512],
                                          modbc[:, 2048 + c * 512:2048 + (c + 1) * 512]),
                            ve.tensor_add(wef[:, c * 512:(c + 1) * 512],
                                          sct[:], sct2[:])][-1])
                    t_wf2.append(t_wc)
                t_ww = P.dma("scalar", waits=[t_wf2[-1]])(
                    lambda sc, l=l, dc=dc: sc.dma_start(
                        out=wefpart[l][dc * 128:(dc + 1) * 128, :], in_=wef[:, :]))
                t_wprev = t_ww
            cc_n += 1
            P.raw("gpsimd", waits=[t_wprev])(
                lambda g, l=l: g.collective_compute(
                    "AllGather", ALU.bypass, replica_groups=[list(range(8))],
                    ins=[wefpart[l].opt()], outs=[wefgath[l].opt()])
                .then_inc(ccs.h, 1))

            t_pb = P.dma("sync", waits=[t_mod[1], t_eb])(
                lambda sy, l=l: sy.dma_start(out=rowa[0:1, :], in_=pathb[l, 0:1, :]))
            t_pb2 = P.step("vector", waits=[t_pb])(
                lambda ve: ve.tensor_mul(bacc2[0:1, :], rowa[0:1, :], rowb[0:1, :]))
            t_pb3 = P.dma("sync", waits=[t_pb2])(
                lambda sy, l=l: sy.dma_start(out=rowa[0:1, :], in_=pathb[l, 1:2, :]))
            t_pb4 = P.step("vector", waits=[t_pb3])(lambda ve, l=l: [
                ve.tensor_mul(rowa[0:1, :], rowa[0:1, :], rowc[0:1, :]),
                ve.tensor_copy(lnst[:, 1:2], lnst[:, 1:2]),
                ve.tensor_add(pathbias_r[0:1, l * 2048:(l + 1) * 2048],
                              bacc2[0:1, :], rowa[0:1, :])][-1])
            t_rowa_prev = t_pb4
            P.flush(block)

        p4_done = P.barrier()
        p4.close()

        p5 = ExitStack()
        wstr = sb(p5, "wstr", [128, 16 * 2048], BF16)
        stg = sb(p5, "stg", [128, 4 * 8192], BF16)
        r1f = sb(p5, "r1f", [128, 4 * 2048], F32)
        thb = r1f
        xn2t = sb(p5, "xn2t", [128, 2048], BF16)
        sqb = stg[:, 8192:16384]
        snb = sb(p5, "snb", [128, 2 * 512], F32)
        g1 = sb(p5, "g1", [128, 512], F32)
        g2 = sb(p5, "g2", [128, 512], F32)
        pp = [ps(p5, f"pp{i}", [128, 512], F32) for i in range(7)]
        trb5 = ps(p5, "trb5", [128, 1024], BF16)
        trb5v = trb5.rearrange("p (k o) -> p k o", o=128)

        stgv = stg.rearrange("p (r dc t) -> p r dc t", dc=16, t=512)
        XN2, AMP, HH, H1 = 0, 1, 2, 3
        r1v = r1f.rearrange("p (s c) -> p s c", c=2048)
        sv = r1f.rearrange("p (jc t) -> p jc t", t=512)

        jgrps5 = [list(range(0, 6)), list(range(6, 12)), list(range(12, 16))]
        prev_ts_ready = None

        def mm_stage(ts, src_r, dst_write, strip_tok, war_tok, dst_kind):
            evtoks = [None] * 16
            for gi, jg in enumerate(jgrps5):
                for jc in jg:
                    w_m = [strip_tok]
                    if war_tok:
                        w_m.append(war_tok)
                    if jc >= 6:
                        if evtoks[jc - 6] is not None:
                            w_m.append(evtoks[jc - 6])
                    t_m = P.step("tensor", waits=w_m)(
                        lambda te, jc=jc, src_r=src_r: [te.matmul(
                            pp[jc % 6][:], wstr[:, dc * 2048 + jc * 128:dc * 2048 + (jc + 1) * 128],
                            stgv[:, src_r, dc, :], start=(dc == 0), stop=(dc == 15))
                            for dc in range(16)][-1] if dst_kind != "bias" else None)
                    evtoks[jc] = dst_write(jc, t_m)
            return evtoks

        for ts in range(4):
            w_r1 = list(p4_done) if ts == 0 else [t_yw]
            t_r1 = P.dma("sync", waits=w_r1, n=4)(
                lambda sy, ts=ts: [sy.dma_start(
                    out=r1v[:, tsub, :],
                    in_=r1scr[ts * 512 + tsub * 128:ts * 512 + (tsub + 1) * 128, :])
                    for tsub in range(4)])
            t_ev5 = None
            t_tr5 = None
            for tsub in range(4):
                xt = r1v[:, tsub, :]
                t_v1 = P.step("vector", waits=[t_r1])(
                    lambda ve, xt=xt: ve.reduce_sum(lnst[:, 0:1], xt, axis=AX.X))
                w_q = [t_r1]
                if t_tr5:
                    w_q.append(t_tr5)
                t_a1 = P.step("scalar", waits=w_q)(
                    lambda ac, xt=xt: ac.activation(xn2t[:], xt, AF.Square, accum_out=lnst[:, 1:2]))
                t_v2 = P.step("vector", waits=[t_a1])(lambda ve: [
                    ve.tensor_scalar(lnst[:, 3:4], lnst[:, 1:2], 1.0 / 2048.0, None, ALU.mult),
                    ve.tensor_scalar(lnst[:, 2:3], lnst[:, 0:1], 1.0 / 2048.0, None, ALU.mult),
                    ve.tensor_copy(lnst[:, 0:1], lnst[:, 0:1]),
                    ve.tensor_mul(lnst[:, 4:5], lnst[:, 2:3], lnst[:, 2:3]),
                    ve.tensor_copy(lnst[:, 0:1], lnst[:, 0:1]),
                    ve.tensor_sub(lnst[:, 5:6], lnst[:, 3:4], lnst[:, 4:5])][-1])
                t_a2 = P.step("scalar", waits=[t_v2])(lambda ac: [
                    ac.activation(lnst[:, 6:7], lnst[:, 5:6], AF.Ln, bias=eps_c[:, 0:1]),
                    ac.activation(lnst[:, 6:7], lnst[:, 6:7], AF.Exp, scale=-0.5)][-1])
                t_v3 = P.step("vector", waits=[t_a2])(lambda ve, xt=xt: [
                    ve.tensor_scalar(lnst[:, 7:8], lnst[:, 2:3], lnst[:, 6:7], -1.0,
                                     ALU.mult, ALU.mult),
                    ve.tensor_copy(lnst[:, 0:1], lnst[:, 0:1]),
                    ve.tensor_scalar(xn2t[:], xt, lnst[:, 6:7], lnst[:, 7:8],
                                     ALU.mult, ALU.add)][-1])
                w_t = [t_v3]
                if ts == 0 and tsub == 0:
                    w_t += p4_done
                if t_ev5:
                    w_t.append(t_ev5)
                t_tA = P.step("tensor", waits=w_t)(
                    lambda te: [te.transpose(trb5v[:, j, :], xn2t[:, j * 128:(j + 1) * 128],
                                             id_bf[:]) for j in range(8)][-1])
                t_eA = P.step("vector", waits=[t_tA])(
                    lambda ve, tsub=tsub: ve.tensor_copy(
                        stgv[:, XN2, 0:8, tsub * 128:(tsub + 1) * 128], trb5v[:]))
                t_tB = P.step("tensor", waits=[t_eA])(
                    lambda te: [te.transpose(trb5v[:, j - 8, :], xn2t[:, j * 128:(j + 1) * 128],
                                             id_bf[:]) for j in range(8, 16)][-1])
                t_tr5 = t_tB
                t_ev5 = P.step("vector", waits=[t_tB])(
                    lambda ve, tsub=tsub: ve.tensor_copy(
                        stgv[:, XN2, 8:16, tsub * 128:(tsub + 1) * 128], trb5v[:]))
            t_xn2 = t_ev5

            w_st = [t_xn2] if ts == 0 else [t_xn2, prev_ts_ready]
            t_stamp = P.dma("gpsimd", waits=w_st, n=4)(
                lambda g: [g.dma_start(
                    out=wstr[:, q * 8192:(q + 1) * 8192]
                    .rearrange("p (dc t) -> p dc t", t=2048),
                    in_=ampw[q * 512:(q + 1) * 512, :].rearrange("(dc p) t -> p dc t", p=128))
                    for q in range(4)])

            def amp_write(jc, t_m):
                return P.step("scalar", waits=[t_m])(
                    lambda ac, jc=jc: ac.activation(
                        stgv[:, AMP, jc, :], pp[jc % 6][:], AF.Sigmoid,
                        bias=ampb_c[:, jc:jc + 1]))
            amp_toks = mm_stage(ts, XN2, amp_write, t_stamp, None, "amp")

            t_stph = P.dma("gpsimd", waits=[amp_toks[15], amp_toks[11]], n=4)(
                lambda g: [g.dma_start(
                    out=wstr[:, q * 8192:(q + 1) * 8192]
                    .rearrange("p (dc t) -> p dc t", t=2048),
                    in_=phw[q * 512:(q + 1) * 512, :].rearrange("(dc p) t -> p dc t", p=128))
                    for q in range(4)])

            tanh_toks = [None] * 16

            def ph_write(jc, t_m):
                w = [t_m, t_r1 if False else t_m]
                tok = P.step("scalar", waits=[t_m])(
                    lambda ac, jc=jc: ac.activation(sv[:, jc, :], pp[jc % 6][:], AF.Tanh,
                                                    bias=phb_c[:, jc:jc + 1]))
                tanh_toks[jc] = tok
                return tok
            ph_toks = mm_stage(ts, XN2, ph_write, t_stph, t_xn2, "ph")

            t_h_toks = []
            t_sin_prev = None
            for jc in range(16):
                t_rn = P.step("vector", waits=[tanh_toks[jc]])(lambda ve, jc=jc: [
                    ve.tensor_scalar(g1[:], sv[:, jc, :], 0.5, 0.125, ALU.mult, ALU.add),
                    ve.tensor_scalar(g2[:], g1[:], RNE, None, ALU.add),
                    ve.tensor_scalar(g2[:], g2[:], -RNE, None, ALU.add),
                    ve.tensor_sub(sv[:, jc, :], g1[:], g2[:])][-1])
                w_sin = [t_rn]
                if t_sin_prev is not None and jc >= 2:
                    w_sin.append(t_h_toks[jc - 2])
                t_sn = P.step("scalar", waits=w_sin)(
                    lambda ac, jc=jc: ac.activation(snb[:, (jc % 2) * 512:(jc % 2 + 1) * 512],
                                                    sv[:, jc, :], AF.Sin, scale=2 * PI))
                t_sin_prev = t_sn
                t_h = P.step("vector", waits=[t_sn])(
                    lambda ve, jc=jc: ve.scalar_tensor_tensor(
                        stgv[:, HH, jc, :], snb[:, (jc % 2) * 512:(jc % 2 + 1) * 512], SQRT2,
                        stgv[:, AMP, jc, :], ALU.mult, ALU.mult))
                t_h_toks.append(t_h)
            t_hlast = t_h_toks[-1]

            h_src = HH
            h_dst = H1
            prev_ready = t_hlast
            for l in range(2):
                t_stc = P.dma("sync", waits=[prev_ready, (ccs, 1 if l == 0 else 3)], n=4)(
                    lambda sy, l=l: [sy.dma_start(
                        out=wstr[:, q * 8192:(q + 1) * 8192]
                        .rearrange("p (dc t) -> p dc t", t=2048),
                        in_=cgath[l][q * 512:(q + 1) * 512, :].rearrange("(dc p) t -> p dc t", p=128))
                        for q in range(4)])

                s_toks = [None] * 16

                def ent_write(jc, t_m, l=l, h_src=h_src):
                    t_b = P.step("tensor", waits=[t_m] if False else [])(
                        lambda te, jc=jc, l=l: te.matmul(
                            pp[jc % 6][:], entbias_r[0:1, l * 2048 + jc * 128:l * 2048 + (jc + 1) * 128],
                            ones_b[0:1, 0:512], start=False, stop=True))
                    tok = P.step("vector", waits=[t_b])(
                        lambda ve, jc=jc, h_src=h_src: ve.tensor_add(
                            sv[:, jc, :], pp[jc % 6][:], stgv[:, h_src, jc, :]))
                    s_toks[jc] = tok
                    return tok
                ent_toks = mm_stage(ts, h_src, ent_write, t_stc, prev_ready, "ent")
                h_out1 = ln_apply = None
                t_sq_toks = []
                for jc in range(16):
                    t_sqj = P.step("scalar", waits=[ent_toks[jc]])(
                        lambda ac, jc=jc: ac.activation(
                            sqb[:, jc * 512:(jc + 1) * 512], sv[:, jc, :], AF.Square))
                    t_sq_toks.append(t_sqj)
                t_sum = P.step("tensor", waits=[ent_toks[15], ent_toks[11], ent_toks[5]])(
                    lambda te: [te.matmul(pp[0][0:1, :], ones_f[:, 0:1], sv[:, jc, :],
                                          start=(jc == 0), stop=(jc == 15)) for jc in range(16)][-1])
                t_sum2 = P.step("tensor", waits=[t_sq_toks[-1]])(
                    lambda te: [te.matmul(pp[1][0:1, :], ones_b[:, 0:1],
                                          sqb[:, jc * 512:(jc + 1) * 512],
                                          start=(jc == 0), stop=(jc == 15)) for jc in range(16)][-1])
                t_rw = P.step("vector", waits=[t_sum, t_sum2])(lambda ve: [
                    ve.tensor_scalar(g1[0:1, :], pp[0][0:1, :], 1.0 / 2048.0, None, ALU.mult),
                    ve.tensor_scalar(g2[0:1, :], pp[1][0:1, :], 1.0 / 2048.0, None, ALU.mult),
                    ve.tensor_mul(sct[0:1, :], g1[0:1, :], g1[0:1, :]),
                    ve.tensor_copy(lnst[:, 1:2], lnst[:, 1:2]),
                    ve.tensor_sub(g2[0:1, :], g2[0:1, :], sct[0:1, :])][-1])
                t_rr = P.step("scalar", waits=[t_rw])(lambda ac: [
                    ac.activation(g2[0:1, :], g2[0:1, :], AF.Ln, bias=eps_c[0:1, 0:1]),
                    ac.activation(g2[0:1, :], g2[0:1, :], AF.Exp, scale=-0.5)][-1])
                t_mr = P.step("vector", waits=[t_rr])(
                    lambda ve: ve.tensor_mul(g1[0:1, :], g1[0:1, :], g2[0:1, :]))
                t_bc1 = P.step("tensor", waits=[t_mr])(lambda te: [
                    te.matmul(pp[2][:], ones_f[0:1, :], g2[0:1, :], start=True, stop=True),
                    te.matmul(pp[3][:], ones_f[0:1, :], g1[0:1, :], start=True, stop=True)][-1])
                ap_toks = []
                for jc in range(16):
                    gcol = lng_c[:, (l * 2) * 16 + jc:(l * 2) * 16 + jc + 1]
                    bcol = lnb_c[:, (l * 2) * 16 + jc:(l * 2) * 16 + jc + 1]
                    t_ap = P.step("vector", waits=[t_bc1])(lambda ve, jc=jc, gcol=gcol, bcol=bcol,
                                                           h_dst=h_dst: [
                        ve.tensor_mul(sct[:], sv[:, jc, :], pp[2][:]),
                        ve.tensor_sub(sct[:], sct[:], pp[3][:]),
                        ve.tensor_scalar(stgv[:, h_dst, jc, :], sct[:], gcol, bcol,
                                         ALU.mult, ALU.add)][-1])
                    ap_toks.append(t_ap)
                t_stw = P.dma("sync", waits=[ap_toks[15], (ccs, 2 if l == 0 else 4)], n=4)(
                    lambda sy, l=l: [sy.dma_start(
                        out=wstr[:, q * 8192:(q + 1) * 8192]
                        .rearrange("p (dc t) -> p dc t", t=2048),
                        in_=wefgath[l][q * 512:(q + 1) * 512, :]
                        .rearrange("(dc p) t -> p dc t", p=128))
                        for q in range(4)])

                sp_toks = [None] * 16

                def path_write(jc, t_m, l=l, h_dst=h_dst):
                    t_b = P.step("tensor", waits=[])(
                        lambda te, jc=jc, l=l: te.matmul(
                            pp[jc % 6][:], pathbias_r[0:1, l * 2048 + jc * 128:l * 2048 + (jc + 1) * 128],
                            ones_b[0:1, 0:512], start=False, stop=True))
                    tok = P.step("vector", waits=[t_b])(
                        lambda ve, jc=jc, h_dst=h_dst: ve.tensor_add(
                            sv[:, jc, :], pp[jc % 6][:], stgv[:, h_dst, jc, :]))
                    sp_toks[jc] = tok
                    return tok
                path_toks = mm_stage(ts, h_dst, path_write, t_stw, ap_toks[-1], "path")
                t_sq_toks2 = []
                for jc in range(16):
                    t_sqj = P.step("scalar", waits=[path_toks[jc]])(
                        lambda ac, jc=jc: ac.activation(
                            sqb[:, jc * 512:(jc + 1) * 512], sv[:, jc, :], AF.Square))
                    t_sq_toks2.append(t_sqj)
                t_sum = P.step("tensor", waits=[path_toks[15], path_toks[11], path_toks[5]])(
                    lambda te: [te.matmul(pp[0][0:1, :], ones_f[:, 0:1], sv[:, jc, :],
                                          start=(jc == 0), stop=(jc == 15)) for jc in range(16)][-1])
                t_sum2 = P.step("tensor", waits=[t_sq_toks2[-1]])(
                    lambda te: [te.matmul(pp[1][0:1, :], ones_b[:, 0:1],
                                          sqb[:, jc * 512:(jc + 1) * 512],
                                          start=(jc == 0), stop=(jc == 15)) for jc in range(16)][-1])
                t_rw = P.step("vector", waits=[t_sum, t_sum2])(lambda ve: [
                    ve.tensor_scalar(g1[0:1, :], pp[0][0:1, :], 1.0 / 2048.0, None, ALU.mult),
                    ve.tensor_scalar(g2[0:1, :], pp[1][0:1, :], 1.0 / 2048.0, None, ALU.mult),
                    ve.tensor_mul(sct[0:1, :], g1[0:1, :], g1[0:1, :]),
                    ve.tensor_copy(lnst[:, 1:2], lnst[:, 1:2]),
                    ve.tensor_sub(g2[0:1, :], g2[0:1, :], sct[0:1, :])][-1])
                t_rr = P.step("scalar", waits=[t_rw])(lambda ac: [
                    ac.activation(g2[0:1, :], g2[0:1, :], AF.Ln, bias=eps_c[0:1, 0:1]),
                    ac.activation(g2[0:1, :], g2[0:1, :], AF.Exp, scale=-0.5)][-1])
                t_mr = P.step("vector", waits=[t_rr])(
                    lambda ve: ve.tensor_mul(g1[0:1, :], g1[0:1, :], g2[0:1, :]))
                t_bc1 = P.step("tensor", waits=[t_mr])(lambda te: [
                    te.matmul(pp[2][:], ones_f[0:1, :], g2[0:1, :], start=True, stop=True),
                    te.matmul(pp[3][:], ones_f[0:1, :], g1[0:1, :], start=True, stop=True)][-1])
                ap2 = []
                for jc in range(16):
                    gcol = lng_c[:, (l * 2 + 1) * 16 + jc:(l * 2 + 1) * 16 + jc + 1]
                    bcol = lnb_c[:, (l * 2 + 1) * 16 + jc:(l * 2 + 1) * 16 + jc + 1]
                    t_ap = P.step("vector", waits=[t_bc1])(lambda ve, jc=jc, gcol=gcol, bcol=bcol,
                                                           h_src=h_src: [
                        ve.tensor_mul(sct[:], sv[:, jc, :], pp[2][:]),
                        ve.tensor_sub(sct[:], sct[:], pp[3][:]),
                        ve.tensor_scalar(stgv[:, h_src, jc, :], sct[:], gcol, bcol,
                                         ALU.mult, ALU.add)][-1])
                    ap2.append(t_ap)
                prev_ready = ap2[-1]
            t_r1b = P.dma("sync", waits=[prev_ready, ap2[0]], n=4)(
                lambda sy, ts=ts: [sy.dma_start(
                    out=r1v[:, tsub, :],
                    in_=r1scr[ts * 512 + tsub * 128:ts * 512 + (tsub + 1) * 128, :])
                    for tsub in range(4)])
            t_fa_prev = None
            fa_toks = []
            for jc in range(16):
                w_ftr = [ap2[jc] if False else ap2[-1], t_r1b]
                if t_fa_prev is not None:
                    w_ftr.append(t_fa_prev)
                t_ftr = P.step("tensor", waits=w_ftr)(
                    lambda te, jc=jc: [te.transpose(trb5v[:, tsub, :],
                                                    stgv[:, HH, jc, tsub * 128:(tsub + 1) * 128],
                                                    id_bf[:]) for tsub in range(4)][-1])
                t_fa = P.step("vector", waits=[t_ftr, t_r1b])(
                    lambda ve, jc=jc: ve.tensor_add(
                        r1v[:, :, jc * 128:(jc + 1) * 128],
                        r1v[:, :, jc * 128:(jc + 1) * 128],
                        trb5v[:, 0:4, :]))
                t_fa_prev = t_fa
                fa_toks.append(t_fa)
            prev_ts_ready = fa_toks[-1]
            t_yw = P.dma("gpsimd", waits=fa_toks)(
                lambda g, ts=ts: g.dma_start(
                    out=y_out[ts * 512:(ts + 1) * 512, :].rearrange("(s p) c -> p s c", p=128),
                    in_=r1v[:]))
            P.flush(block)

        yw_tok = (P.qsem["gpsimd"], P.qsem["gpsimd"].n)

        @block.sync
        def _(sy):
            sy.wait_ge(yw_tok[0].h, yw_tok[1])

        p5.close()
    return nc


def _host_inputs(inputs, core):
    b, hf = core // 2, core % 2
    f32 = np.float32
    x = np.asarray(inputs["L0_attn_x"], dtype=f32)
    d = {}
    d["x0"] = np.ascontiguousarray(x[b, hf * T_:(hf + 1) * T_, :])
    d["x1"] = np.ascontiguousarray(x[b, (1 - hf) * T_:(2 - hf) * T_, :])
    d["wqkv"] = np.ascontiguousarray(np.asarray(inputs["L0_attn_w_qkv"], f32))
    d["wo_w"] = np.ascontiguousarray(np.asarray(inputs["L0_attn_wo"], f32))
    lam = np.stack([np.asarray(inputs["L0_attn_lambda_q1"], f32),
                    np.asarray(inputs["L0_attn_lambda_k1"], f32),
                    np.asarray(inputs["L0_attn_lambda_q2"], f32),
                    np.asarray(inputs["L0_attn_lambda_k2"], f32)], axis=1)
    d["lamv"] = np.ascontiguousarray(lam)
    gam = np.asarray(inputs["L0_attn_gamma"], f32)
    bet = np.asarray(inputs["L0_attn_beta"], f32)
    d["grep"] = np.repeat(gam[hf * 1024:(hf + 1) * 1024], 2)[None, :].copy()
    d["brep"] = np.repeat(bet[hf * 1024:(hf + 1) * 1024], 2)[None, :].copy()
    d["ampw"] = np.ascontiguousarray(np.asarray(inputs["L0_ffn_amp_w"], f32))
    d["ampb"] = np.asarray(inputs["L0_ffn_amp_b"], f32)[None, :].copy()
    d["phw"] = np.ascontiguousarray(np.asarray(inputs["L0_ffn_ph_w"], f32))
    d["phb"] = np.asarray(inputs["L0_ffn_ph_b"], f32)[None, :].copy()
    entw = np.asarray(inputs["L0_ffn_ent_w"], f32)
    entb = np.asarray(inputs["L0_ffn_ent_b"], f32)
    r0 = core * 256
    d["entwT"] = np.ascontiguousarray(np.stack(
        [np.concatenate([entw[l, r0:r0 + 256, :], entb[l:l + 1, :]], axis=0).T
         for l in range(2)], axis=0))
    d["entpw"] = np.ascontiguousarray(np.asarray(inputs["L0_ffn_ent_pw"], f32))
    d["entpb"] = np.ascontiguousarray(np.asarray(inputs["L0_ffn_ent_pb"], f32))
    d["pathwS"] = np.ascontiguousarray(
        np.asarray(inputs["L0_ffn_path_w"], f32)[:, :, r0:r0 + 256, :])
    d["pathb"] = np.ascontiguousarray(np.asarray(inputs["L0_ffn_path_b"], f32))
    d["phasev"] = np.ascontiguousarray(np.asarray(inputs["L0_ffn_phase"], f32))
    d["l3g"] = np.ascontiguousarray(np.asarray(inputs["L0_ffn_ln3_g"], f32))
    d["l3b"] = np.ascontiguousarray(np.asarray(inputs["L0_ffn_ln3_b"], f32))
    d["l4g"] = np.ascontiguousarray(np.asarray(inputs["L0_ffn_ln4_g"], f32))
    d["l4b"] = np.ascontiguousarray(np.asarray(inputs["L0_ffn_ln4_b"], f32))

    pos = np.concatenate([np.arange(hf * T_, (hf + 1) * T_),
                          np.arange((1 - hf) * T_, (2 - hf) * T_)]).astype(np.float64)
    inv = 1.0 / 10000.0 ** (np.arange(0, 128, 2, dtype=np.float64) / 128.0)
    ang = pos[None, :] * inv[:, None]
    cos = np.cos(ang); sin = np.sin(ang)
    cos2 = np.concatenate([cos, cos], axis=0).astype(f32)
    sin2 = np.concatenate([sin, sin], axis=0).astype(f32)
    d["ropetab"] = np.ascontiguousarray(
        np.stack([cos2 * np.float32(SCALER), sin2 * np.float32(SCALER), cos2, sin2], axis=0))

    mb = np.zeros((4, 128, 512), dtype=f32)
    for r in range(4):
        kpos = r * 128 + np.arange(128)
        mb[r] = (kpos[:, None] > np.arange(512)[None, :]).astype(f32)
    d["mbands"] = mb.astype(BF)
    d["mo"] = np.array([[1.0 if hf == 0 else 0.0]], dtype=f32)
    d["idbf"] = np.eye(128, dtype=BF)
    d["onesf_in"] = np.ones((128, 128), dtype=f32)
    d["onesb_in"] = np.ones((128, 512), dtype=BF)
    mt = np.zeros((128, 128), dtype=f32)
    for j in range(64):
        mt[j, j + 64] = 1.0
    for j in range(64, 128):
        mt[j, j - 64] = -1.0
    d["mtrot"] = mt.astype(BF)
    return d


def kernel(**inputs):
    if "nc" not in _CACHE:
        _CACHE["nc"] = build_program()
    nc = _CACHE["nc"]
    in_maps = [_host_inputs(inputs, c) for c in range(8)]
    res = run_bass_kernel_spmd(nc, in_maps, core_ids=list(range(8)))
    out = np.empty((4, 4096, 2048), dtype=np.float32)
    for c in range(8):
        b, hf = c // 2, c % 2
        out[b, hf * T_:(hf + 1) * T_, :] = res.results[c]["y"]
    return out


if __name__ == "__main__":
    import reference as _r
    ins = _r.setup_inputs(0)
    y = kernel(**{k: np.asarray(v) for k, v in ins.items()})
    ref = np.asarray(_r.reference(**ins))
    num = float(np.mean((y - ref) ** 2)); den = float(np.mean(ref ** 2))
    print("rel var ratio:", num / den)
